# version 02
# baseline (speedup 1.0000x reference)
import numpy as np
import ml_dtypes
from contextlib import ExitStack

import concourse.bass as bass
import concourse.mybir as mybir
from concourse.bass_utils import run_bass_kernel_spmd

F32 = mybir.dt.float32
BF16 = mybir.dt.bfloat16
I32 = mybir.dt.int32
AF = mybir.ActivationFunctionType
ALU = mybir.AluOpType

L = 4096
HID = 4096
TST = 512
NST = L // TST
CS = 256
NG = 4
HPG = 16
NHC = NG * HPG
DIN = 8192
GTOT = 8
NSD = 128
WIP_R = 9344
GROW = 2304
EPS = 1e-5
BIG = 1.0e9

SYNC, TEN, VEC, ACT, GPS = "sync", "tensor", "vector", "scalar", "gpsimd"
ENGS = [SYNC, TEN, VEC, ACT, GPS]


class Sem:
    def __init__(self, h):
        self.h = h
        self.n = 0


def build_nc(debug=False):
    nc = bass.Bass("TRN2", target_bir_lowering=False)

    def ein(name, shape, dt):
        return nc.dram_tensor(name, shape, dt, kind="ExternalInput").ap()

    hs = ein("hs", [L, HID], F32)
    xres = ein("xres", [L, 2048], F32)
    wip = ein("wip", [WIP_R, HID], F32)
    wop = ein("wop", [HID, 4096], F32)
    wq = ein("wq", [2048, HID], I32)
    fbias = ein("fbias", [1, 2048], F32)
    scale_i = ein("scale_i", [1, 1], F32)
    zp_i = ein("zp_i", [1, 1], F32)
    convw_i = ein("convw_i", [128, 160], F32)
    convb_i = ein("convb_i", [128, 40], F32)
    nw_i = ein("nw_i", [128, 32], F32)
    dtb_i = ein("dtb_i", [64, 1], F32)
    alog_i = ein("alog_i", [64, 1], F32)
    dpair_i = ein("dpair_i", [2, 32], F32)
    ltrow0_i = ein("ltrow0_i", [128, 256], BF16)
    ltrow1_i = ein("ltrow1_i", [128, 256], BF16)
    tri16_i = ein("tri16_i", [128, 128], BF16)
    ones128_i = ein("ones128_i", [128, 128], BF16)
    maskbig_i = ein("maskbig_i", [128, 512], F32)
    id16_i = ein("id16_i", [128, 128], BF16)
    idf_i = ein("idf_i", [128, 128], F32)
    onesf_i = ein("onesf_i", [1, 128], F32)
    onesrow16_i = ein("onesrow16_i", [1, 128], BF16)
    onescol16_i = ein("onescol16_i", [128, 1], BF16)
    updown_i = ein("updown_i", [2, 128], F32)
    fconst_i = ein("fconst_i", [128, 4], F32)

    outd = nc.dram_tensor("out", [L, 2048], F32, kind="ExternalOutput").ap()
    if debug:
        d_xn = nc.dram_tensor("d_xn", [128, 16384], BF16, kind="ExternalOutput").ap()
        d_dt = nc.dram_tensor("d_dt", [64, 1024], F32, kind="ExternalOutput").ap()
        d_pc = nc.dram_tensor("d_pc", [128, 1024], F32, kind="ExternalOutput").ap()
        d_xp = nc.dram_tensor("d_xp", [128, 5150], BF16, kind="ExternalOutput").ap()
        d_sg = nc.dram_tensor("d_sg", [128, 4096], BF16, kind="ExternalOutput").ap()
        d_yf = nc.dram_tensor("d_yf", [128, 16384], BF16, kind="ExternalOutput").ap()
        d_sp = nc.dram_tensor("d_sp", [1024, 2048], BF16, kind="ExternalOutput").ap()
        d_ss = nc.dram_tensor("d_ss", [512, 2048], BF16, kind="ExternalOutput").ap()
        d_st = nc.dram_tensor("d_st", [128, 4096], F32, kind="ExternalOutput").ap()
        d_yg = nc.dram_tensor("d_yg", [128, 2048], BF16, kind="ExternalOutput").ap()
        d_nm = nc.dram_tensor("d_nm", [128, 1024], F32, kind="ExternalOutput").ap()
        d_wf = nc.dram_tensor("d_wf", [256, 2048], BF16, kind="ExternalOutput").ap()
        d_wq = nc.dram_tensor("d_wq", [2048, 128], BF16, kind="ExternalOutput").ap()
        d_sq = nc.dram_tensor("d_sq", [128, 4096], BF16, kind="ExternalOutput").ap()
        d_cb = nc.dram_tensor("d_cb", [128, 1536], F32, kind="ExternalOutput").ap()
        d_wfull = nc.dram_tensor("d_wfull", [4096, 2048], BF16, kind="ExternalOutput").ap()
        d_wfull2 = nc.dram_tensor("d_wfull2", [2048, 2048], BF16, kind="ExternalOutput").ap()
        d_lt = nc.dram_tensor("d_lt", [128, 512], BF16, kind="ExternalOutput").ap()
        d_gl = nc.dram_tensor("d_gl", [128, 512], BF16, kind="ExternalOutput").ap()
        d_cs = nc.dram_tensor("d_cs", [128, 512], BF16, kind="ExternalOutput").ap()

    wip_bf = nc.dram_tensor("wip_bf", [WIP_R, HID], BF16).ap()
    wipT = nc.dram_tensor("wipT", [HID, WIP_R], BF16).ap()
    wop_bf = nc.dram_tensor("wop_bf", [HID, 4096], BF16).ap()
    wopT = nc.dram_tensor("wopT", [HID, 4096], BF16).ap()
    wq_bf = nc.dram_tensor("wq_bf", [2048, HID], BF16).ap()
    wfT = nc.dram_tensor("wfT", [HID, 2048], BF16).ap()
    fb_bf = nc.dram_tensor("fb_bf", [1, 2048], BF16).ap()
    s_part = nc.dram_tensor("s_part", [NST * 1024, 2048], BF16).ap()
    s_sum = nc.dram_tensor("s_sum", [NST * 512, 2048], BF16).ap()

    st_ctx = ExitStack()

    def sbuf(name, shape, dt):
        return st_ctx.enter_context(nc.sbuf_tensor(name, shape, dt))

    xn = sbuf("xn", [128, 32 * 512], BF16)
    yf = sbuf("yf", [128, 32 * 512], BF16)
    xtm = sbuf("xtm", [128, 4096], F32)
    xntm = sbuf("xntm", [128, 4096], BF16)
    scanreg = sbuf("scanreg", [128, 16384], BF16)
    wsec = sbuf("wsec", [128, 8192], BF16)
    xpre = sbuf("xpre", [128, 10 * 515], BF16)
    tails = sbuf("tails", [128, 120], BF16)
    sgb = sbuf("sgb", [128, 8 * 512], BF16)
    wdt = sbuf("wdt", [128, 32 * 64], BF16)
    dtsp = sbuf("dtsp", [64, 512], F32)
    tmp1 = sbuf("tmp1", [64, 512], F32)
    tmp2 = sbuf("tmp2", [64, 512], F32)
    adt = sbuf("adt", [64, 512], F32)
    adttm = sbuf("adttm", [128, 256], BF16)
    dttm = sbuf("dttm", [128, 256], F32)
    acstm = sbuf("acstm", [128, 256], F32)
    nacstm = sbuf("nacstm", [128, 256], F32)
    ddtm = sbuf("ddtm", [128, 256], BF16)
    ddt1 = sbuf("ddt1", [128, 64], F32)
    ddt2 = sbuf("ddt2", [128, 64], F32)
    eatb = sbuf("eatb", [128, 128], F32)
    rstdr = sbuf("rstdr", [1, 512], F32)
    statef = sbuf("statef", [128, NHC * 64], F32)
    ltrow0 = sbuf("ltrow0", [128, 256], BF16)
    ltrow1 = sbuf("ltrow1", [128, 256], BF16)
    tri16 = sbuf("tri16", [128, 128], BF16)
    ones128 = sbuf("ones128", [128, 128], BF16)
    maskbig = sbuf("maskbig", [128, 512], F32)
    id16 = sbuf("id16", [128, 128], BF16)
    idf = sbuf("idf", [128, 128], F32)
    onesf = sbuf("onesf", [1, 128], F32)
    onesrow16 = sbuf("onesrow16", [1, 128], BF16)
    onescol16 = sbuf("onescol16", [128, 1], BF16)
    updown = sbuf("updown", [2, 128], F32)
    dpair = sbuf("dpair", [2, 32], F32)
    nwc = sbuf("nwc", [128, 32], F32)
    cwc = sbuf("cwc", [128, 160], F32)
    cbc = sbuf("cbc", [128, 40], F32)
    dtbc = sbuf("dtbc", [64, 1], F32)
    alogc = sbuf("alogc", [64, 1], F32)
    expalog = sbuf("expalog", [64, 1], F32)
    dexp2 = sbuf("dexp2", [128, 32], F32)
    scl = sbuf("scl", [1, 1], F32)
    zpv = sbuf("zpv", [1, 1], F32)
    scaleb = sbuf("scaleb", [128, 1], F32)
    zpb = sbuf("zpb", [128, 1], F32)
    biascol = sbuf("biascol", [128, 1], F32)
    fb16 = sbuf("fb16", [1, 2048], BF16)
    stats = sbuf("stats", [128, 48], F32)
    mv = sbuf("mv", [128, 2], F32)
    rstdc = sbuf("rstdc", [128, 1], F32)
    nbias = sbuf("nbias", [128, 1], F32)
    lntmp = sbuf("lntmp", [128, 1], F32)
    fconst = sbuf("fconst", [128, 4], F32)

    xtmc = scanreg[:, 0:2048]
    xwv = scanreg[:, 2048:4096]
    ygb = scanreg[:, 4096:6144]
    glt = scanreg[:, 6144:6656]
    lts = scanreg[:, 6656:7168]
    ttsf = scanreg[:, 7168:9216].bitcast(F32)
    csdv = scanreg[:, 9216:9472]
    sdtv = scanreg[:, 9472:9728]
    stab = scanreg[:, 9728:10752]
    btm = scanreg[:, 10752:11008]
    sqt = scanreg[:, 11008:11264]
    convacc = scanreg[:, 11264:13312].bitcast(F32)
    rhsb = scanreg[:, 13312:14336]
    wbig = [scanreg[:, 0:8192], scanreg[:, 8192:16384]]
    sstage = sgb[:, 0:1024]
    gbuf = sgb[:, 1024:2048]
    ssumb = sgb[:, 2048:3072]
    obuf = xpre[:, 0:2048].bitcast(F32)

    psA = st_ctx.enter_context(nc.psum_tensor("psA", [128, 512], F32))
    psB = st_ctx.enter_context(nc.psum_tensor("psB", [128, 512], F32))
    psAC = [st_ctx.enter_context(nc.psum_tensor("psAC0", [128, 512], F32)),
            st_ctx.enter_context(nc.psum_tensor("psAC1", [128, 512], F32))]
    psG = [st_ctx.enter_context(nc.psum_tensor("psG0", [128, 512], F32)),
           st_ctx.enter_context(nc.psum_tensor("psG1", [128, 512], F32))]
    psY = st_ctx.enter_context(nc.psum_tensor("psY", [128, 512], F32))
    psX = st_ctx.enter_context(nc.psum_tensor("psX", [128, 512], F32))
    psXb = psX[:].bitcast(BF16)

    sems = [Sem(st_ctx.enter_context(nc.semaphore(s)))
            for s in ["sL", "sS", "sC", "sCC", "sP", "sV", "sA"]]
    sL, sS, sC, sCC, sP, sV, sA = sems

    P = {e: [] for e in ENGS}

    def op(eng, meth, *a, inc=None, incn=1, **kw):
        if inc is not None:
            inc.n += incn
            h, n = inc.h, incn
        else:
            h, n = None, 0

        def fn(e, meth=meth, a=a, kw=kw, h=h, n=n):
            ins = getattr(e, meth)(*a, **kw)
            if h is not None:
                ins.then_inc(h, n)
        P[eng].append(fn)
        return inc.n if inc else None

    def wt(eng, sem, val=None):
        v = sem.n if val is None else val

        def fn(e, h=sem.h, v=v):
            e.wait_ge(h, v)
        P[eng].append(fn)

    def dma(eng, out, in_, sem, **kw):
        return op(eng, "dma_start", inc=sem, incn=16, out=out, in_=in_, **kw)

    for dst, src in [
        (ltrow0[:], ltrow0_i), (ltrow1[:], ltrow1_i), (tri16[:], tri16_i),
        (ones128[:], ones128_i), (maskbig[:], maskbig_i), (id16[:], id16_i),
        (idf[:], idf_i), (onesf[:], onesf_i), (onesrow16[:], onesrow16_i),
        (onescol16[:], onescol16_i), (updown[:], updown_i), (dpair[:], dpair_i),
        (nwc[:], nw_i), (cwc[:], convw_i), (cbc[:], convb_i),
        (dtbc[:], dtb_i), (alogc[:], alog_i), (scl[:], scale_i), (zpv[:], zp_i),
        (fconst[:], fconst_i),
    ]:
        dma(SYNC, dst, src, sL)
    consts_loaded = sL.n

    op(VEC, "memset", statef[:], 0.0)
    op(VEC, "memset", tails[:], 0.0, inc=sV)
    memset_v = sV.n

    wt(ACT, sL, consts_loaded)
    op(ACT, "activation", expalog[:], alogc[:], AF.Exp, inc=sA)
    wt(TEN, sL, consts_loaded)
    psXf = psX[:]
    op(TEN, "matmul", psXf[:, 0:1], onesf[:], scl[:], start=True, stop=True)
    op(TEN, "matmul", psXf[:, 1:2], onesf[:], zpv[:], start=True, stop=True)
    op(TEN, "matmul", psXf[:, 2:34], updown[:], dpair[:], start=True, stop=True,
       inc=sP)
    wt(ACT, sP)
    op(ACT, "copy", scaleb[:], psXf[:, 0:1])
    op(ACT, "copy", zpb[:], psXf[:, 1:2])
    op(ACT, "copy", dexp2[:], psXf[:, 2:34], inc=sA)
    psx_setup_done = sA.n
    wt(VEC, sA, psx_setup_done)
    op(VEC, "tensor_scalar", biascol[:], scaleb[:], zpb[:], -1.0,
       ALU.mult, ALU.mult, inc=sV)
    biascol_v = sV.n

    for i in range(8):
        dma(GPS, wip_bf[i * 1168:(i + 1) * 1168, :], wip[i * 1168:(i + 1) * 1168, :], sC)
    cast_wip = sC.n
    for i in range(4):
        dma(GPS, wop_bf[i * 1024:(i + 1) * 1024, :], wop[i * 1024:(i + 1) * 1024, :], sC)
    cast_wop = sC.n
    for i in range(2):
        dma(GPS, wq_bf[i * 1024:(i + 1) * 1024, :], wq[i * 1024:(i + 1) * 1024, :], sC)
    cast_wq = sC.n
    dma(GPS, fb_bf, fbias, sC)
    cast_fb = sC.n

    wt(SYNC, sC, cast_fb)
    dma(SYNC, fb16[:], fb_bf, sL)

    def prepass_transpose(wsrc, wdst, nob, cast_v, dequant=False):
        cbuf = yf[:, 0:nob * 128]
        rbuf = xn[:, 0:nob * 128]
        rbuf2 = xn[:, 8192:8192 + nob * 128] if dequant else None
        wt(SYNC, sC, cast_v)
        for kc in range(32):
            wt(SYNC, sP)
            dma(SYNC, cbuf,
                wsrc[:, kc * 128:(kc + 1) * 128].rearrange(
                    "(ob p) k -> p ob k", p=128), sL)
            ldv = sL.n
            wt(TEN, sL, ldv)
            nbat = (nob + 7) // 8
            for bq in range(nbat):
                o0 = bq * 8
                o1 = min(o0 + 8, nob)
                wt(TEN, sV)
                for ob in range(o0, o1):
                    op(TEN, "transpose", psXb[:, (ob - o0) * 128:(ob - o0 + 1) * 128],
                       cbuf[:, ob * 128:(ob + 1) * 128], id16[:],
                       inc=(sP if ob == o1 - 1 else None))
                wt(VEC, sP)
                if bq == 0:
                    wt(VEC, sS)
                op(VEC, "tensor_copy", rbuf[:, o0 * 128:o1 * 128],
                   psXb[:, 0:(o1 - o0) * 128], inc=sV)
            if dequant:
                wt(ACT, sV)
                op(ACT, "activation", rbuf2, rbuf, AF.Identity,
                   bias=biascol[:], scale=scaleb[:], inc=sA)
                wt(SYNC, sA)
                dma(SYNC, wdst[kc * 128:(kc + 1) * 128, :], rbuf2, sS)
            else:
                wt(ACT, sV)
                dma(ACT, wdst[kc * 128:(kc + 1) * 128, :], rbuf, sS)

    wt(ACT, sV, biascol_v)
    prepass_transpose(wip_bf, wipT, 73, cast_wip)
    prepass_transpose(wop_bf, wopT, 32, cast_wop)
    prepass_transpose(wq_bf, wfT, 16, cast_wq, dequant=True)
    prepass_stores = sS.n
    if debug:
        wt(GPS, sS, prepass_stores)
        dma(GPS, d_wf[0:128, :], wfT[0:128, :], sS)
        dma(GPS, d_wf[128:256, :], wfT[1024:1152, :], sS)
        dma(GPS, d_wq, wq_bf[:, 0:128], sS)
        dma(GPS, d_wfull2, wq_bf[:, 2048:4096], sS)
        dma(GPS, d_wfull, wfT, sS)

    for e in ENGS:
        wt(e, sS, prepass_stores)
        wt(e, sV, memset_v)

    v_yf_consumed = 0
    v_xn_consumed = 0
    v_sg_free = 0
    v_obuf_free = 0
    v_scanreg_pe = 0
    xtm_guard_a = 0
    xtm_guard_v = 0
    wsec_guard = [0, 0]
    wbig_guard = [0, 0]
    sstage_guard = [0, 0, 0, 0]
    gslot_guard = [0, 0, 0, 0]
    obuf_guard = [0, 0, 0, 0]
    evac_done = [0, 0]
    dtmm_prev = [0]
    rstd_guard = [0]

    for st in range(NST):
        r0 = st * TST
        for tt in range(4):
            rows = slice(r0 + tt * 128, r0 + tt * 128 + 128)
            wt(SYNC, sA, xtm_guard_a)
            wt(SYNC, sV, xtm_guard_v)
            dma(SYNC, xtm[:, 0:2048], hs[rows, 0:2048], sL)
            dma(SYNC, xtm[:, 2048:4096], hs[rows, 2048:4096], sL)
            xldv = sL.n
            wt(VEC, sL, xldv)
            wt(VEC, sA)
            for i in range(8):
                op(VEC, "bn_stats", stats[:, 6 * i:6 * i + 6],
                   xtm[:, 512 * i:512 * i + 512])
            op(VEC, "bn_aggr", mv[:], stats[:], inc=sV)
            xtm_guard_v = sV.n
            wt(ACT, sV)
            op(ACT, "activation", lntmp[:], mv[:, 1:2], AF.Ln, bias=fconst[:, 0:1])
            op(ACT, "activation", rstdc[:], lntmp[:], AF.Exp, scale=fconst[:, 3:4], inc=sA)
            wt(VEC, sA)
            op(VEC, "tensor_scalar", nbias[:], mv[:, 0:1], rstdc[:], -1.0,
               ALU.mult, ALU.mult, inc=sV)
            wt(ACT, sV)
            wt(ACT, sP)
            op(ACT, "activation", xntm[:, 0:2048], xtm[:, 0:2048], AF.Identity,
               bias=nbias[:], scale=rstdc[:])
            op(ACT, "activation", xntm[:, 2048:4096], xtm[:, 2048:4096], AF.Identity,
               bias=nbias[:], scale=rstdc[:], inc=sA)
            xtm_guard_a = sA.n
            xn_ready_a = sA.n
            for bq in range(4):
                wt(TEN, sA, xn_ready_a)
                wt(TEN, sV)
                for k in range(8):
                    kc = bq * 8 + k
                    op(TEN, "transpose", psXb[:, k * 128:(k + 1) * 128],
                       xntm[:, kc * 128:(kc + 1) * 128], id16[:],
                       inc=(sP if k == 7 else None))
                if st > 0 and tt == 0 and bq == 0:
                    wt(VEC, sP, v_xn_consumed)
                wt(VEC, sP)
                xnv = xn[:].rearrange("p (kc t) -> p kc t", t=512)
                op(VEC, "tensor_copy",
                   xnv[:, bq * 8:(bq + 1) * 8, tt * 128:(tt + 1) * 128],
                   psXb[:].rearrange("p (k t) -> p k t", t=128), inc=sV)
        xn_done_v = sV.n
        if debug and st == 0:
            wt(ACT, sV, xn_done_v)
            dma(ACT, d_xn, xn[:], sS)
            for _e in ENGS:
                wt(_e, sS, sS.n)

        wt(SYNC, sP, dtmm_prev[0])
        dma(SYNC, wdt[:], wipT[:, 9216:9280].rearrange("(kc p) o -> p kc o", p=128), sL)
        wdt_v = sL.n
        wt(TEN, sL, wdt_v)
        wt(TEN, sV, xn_done_v)
        wt(TEN, sA, evac_done[0])
        for kc in range(32):
            op(TEN, "matmul", psA[0:64, :], wdt[:, kc * 64:(kc + 1) * 64],
               xn[:, kc * 512:(kc + 1) * 512], start=(kc == 0), stop=(kc == 31),
               inc=(sP if kc == 31 else None))
        dtmm_prev[0] = sP.n
        wt(ACT, sP)
        op(ACT, "activation", tmp1[:], psA[0:64, :], AF.Relu, bias=dtbc[:])
        op(ACT, "activation", tmp2[:], psA[0:64, :], AF.Abs, bias=dtbc[:], inc=sA)
        evac_done[0] = sA.n
        op(ACT, "activation", dtsp[:], tmp2[:], AF.Exp, scale=-1.0)
        op(ACT, "activation", tmp2[:], dtsp[:], AF.Ln, bias=fconst[0:64, 1:2], inc=sA)
        wt(VEC, sA)
        op(VEC, "tensor_tensor", dtsp[:], tmp1[:], tmp2[:], ALU.add)
        op(VEC, "tensor_scalar", adt[:], dtsp[:], expalog[:], -1.0,
           ALU.mult, ALU.mult, inc=sV)

        for ch in range(2):
            c2 = ch * 2
            wt(TEN, sV)
            for jt in range(2):
                cidx = c2 + jt
                csl = slice(ch * 256 + jt * 128, ch * 256 + jt * 128 + 128)
                op(TEN, "transpose", psXf[:, 0:64], dtsp[0:64, csl],
                   idf[0:64, 0:64])
                op(TEN, "transpose", psXf[:, 64:128], adt[0:64, csl],
                   idf[0:64, 0:64], inc=sP)
                wt(ACT, sP)
                op(ACT, "copy", dttm[:, cidx * 64:(cidx + 1) * 64], psXf[:, 0:64])
                op(ACT, "copy", adttm[:, cidx * 64:(cidx + 1) * 64],
                   psXf[:, 64:128], inc=sA)
                wt(TEN, sA)
            op(TEN, "matmul", psXf[:, 0:64], tri16[:],
               adttm[:, c2 * 64:(c2 + 1) * 64], start=True, stop=True)
            op(TEN, "matmul", psXf[:, 64:128], ones128[:],
               adttm[:, c2 * 64:(c2 + 1) * 64], start=True, stop=False)
            op(TEN, "matmul", psXf[:, 64:128], tri16[:],
               adttm[:, (c2 + 1) * 64:(c2 + 2) * 64], start=False, stop=True)
            op(TEN, "matmul", psXf[:, 128:192], ones128[:],
               adttm[:, c2 * 64:(c2 + 1) * 64], start=True, stop=False)
            op(TEN, "matmul", psXf[:, 128:192], ones128[:],
               adttm[:, (c2 + 1) * 64:(c2 + 2) * 64], start=False, stop=True,
               inc=sP)
            wt(ACT, sP)
            op(ACT, "copy", acstm[:, c2 * 64:(c2 + 2) * 64], psXf[:, 0:128])
            op(ACT, "activation", eatb[:, ch * 64:(ch + 1) * 64],
               psXf[:, 128:192], AF.Exp, inc=sA)
            wt(VEC, sA)
            op(VEC, "tensor_scalar", nacstm[:, c2 * 64:(c2 + 2) * 64],
               acstm[:, c2 * 64:(c2 + 2) * 64], -1.0, None, ALU.mult)
            for jt in range(2):
                cidx = c2 + jt
                op(VEC, "tensor_tensor", ddt1[:], psXf[:, 128:192],
                   acstm[:, cidx * 64:(cidx + 1) * 64], ALU.subtract, inc=sV)
                wt(ACT, sV)
                op(ACT, "activation", ddt2[:], ddt1[:], AF.Exp, inc=sA)
                wt(VEC, sA)
                op(VEC, "tensor_tensor", ddtm[:, cidx * 64:(cidx + 1) * 64],
                   ddt2[:], dttm[:, cidx * 64:(cidx + 1) * 64], ALU.mult, inc=sV)
            wt(TEN, sV)
        if debug and st == 0:
            wt(ACT, sV)
            dma(ACT, d_dt[:, 0:512], dtsp[:], sS)
            dma(ACT, d_dt[:, 512:1024], adt[:], sS)
            dma(ACT, d_pc[:, 0:256], acstm[:], sS)
            dma(ACT, d_pc[:, 256:384], eatb[:], sS)
            dma(ACT, d_pc[:, 384:640], dttm[:], sS)
            dma(ACT, d_pc[:, 640:768], adttm[:].bitcast(F32), sS)
            dma(ACT, d_pc[:, 768:896], ddtm[:].bitcast(F32), sS)
            for _e in ENGS:
                wt(_e, sS, sS.n)

        for g in range(NG):
            gcol0 = g * GROW
            if st > 0 and g == 0:
                wt(ACT, sP, v_scanreg_pe)
                wt(VEC, sP, v_scanreg_pe)
            for ot in range(18):
                b = ot % 2
                wt(SYNC, sP, wsec_guard[b])
                c0 = gcol0 + ot * 128
                dma(SYNC, wsec[:, b * 4096:(b + 1) * 4096],
                    wipT[:, c0:c0 + 128].rearrange("(kc p) o -> p kc o", p=128), sL)
                wv = sL.n
                ps = psA if b == 0 else psB
                wt(TEN, sL, wv)
                wt(TEN, sA, evac_done[b])
                for kc in range(32):
                    op(TEN, "matmul", ps[:],
                       wsec[:, b * 4096 + kc * 128: b * 4096 + (kc + 1) * 128],
                       xn[:, kc * 512:(kc + 1) * 512],
                       start=(kc == 0), stop=(kc == 31),
                       inc=(sP if kc == 31 else None))
                wsec_guard[b] = sP.n
                wt(ACT, sP)
                if ot == 0:
                    wt(ACT, sV)
                if ot < 8:
                    if st > 0 and g == 0 and ot == 0:
                        wt(ACT, sV, v_sg_free)
                    op(ACT, "activation", sgb[:, ot * 512:(ot + 1) * 512], ps[:],
                       AF.Silu, inc=sA)
                else:
                    cot = ot - 8
                    if st > 0 and g == 0 and ot == 8:
                        wt(ACT, sS, v_obuf_free)
                    op(ACT, "copy", xpre[:, cot * 515 + 3:cot * 515 + 515], ps[:],
                       inc=sA)
                evac_done[b] = sA.n
            wt(VEC, sA)
            for cot in range(10):
                blk = cot * 515
                ci = g * 10 + cot
                wt(VEC, sA)
                op(VEC, "tensor_copy", xpre[:, blk:blk + 3],
                   tails[:, ci * 3:ci * 3 + 3])
                op(VEC, "tensor_scalar", convacc[:, 0:512],
                   xpre[:, blk + 3:blk + 515],
                   cwc[:, ci * 4 + 3:ci * 4 + 4], cbc[:, ci:ci + 1],
                   ALU.mult, ALU.add)
                op(VEC, "scalar_tensor_tensor", convacc[:, 512:1024],
                   xpre[:, blk + 2:blk + 514], cwc[:, ci * 4 + 2:ci * 4 + 3],
                   convacc[:, 0:512], ALU.mult, ALU.add)
                op(VEC, "scalar_tensor_tensor", convacc[:, 0:512],
                   xpre[:, blk + 1:blk + 513], cwc[:, ci * 4 + 1:ci * 4 + 2],
                   convacc[:, 512:1024], ALU.mult, ALU.add)
                op(VEC, "scalar_tensor_tensor", convacc[:, 512:1024],
                   xpre[:, blk + 0:blk + 512], cwc[:, ci * 4 + 0:ci * 4 + 1],
                   convacc[:, 0:512], ALU.mult, ALU.add)
                op(VEC, "tensor_copy", tails[:, ci * 3:ci * 3 + 3],
                   xpre[:, blk + 512:blk + 515], inc=sV)
                wt(ACT, sV)
                op(ACT, "activation", xpre[:, blk + 3:blk + 515],
                   convacc[:, 512:1024], AF.Silu, inc=sA)
            conv_done = sA.n
            if debug and st == 0 and g == 0:
                dma(ACT, d_xp, xpre[:], sS)
                dma(ACT, d_sg, sgb[:], sS)
                for _e in ENGS:
                    wt(_e, sS, sS.n)

            for ch in range(2):
                ccol = ch * 256
                ccsl = slice(9 * 515 + 3 + ccol, 9 * 515 + 3 + ccol + 256)
                wt(ACT, sV)
                wt(ACT, sP)
                op(ACT, "copy", stab, statef[:, g * 1024:(g + 1) * 1024], inc=sA)
                stab_v = sA.n
                wt(TEN, sA, conv_done)
                wt(TEN, sV)
                for jt in range(2):
                    for kx in range(8):
                        op(TEN, "transpose", psXb[:, kx * 128:(kx + 1) * 128],
                           xpre[:, kx * 515 + 3 + ccol + jt * 128:
                                kx * 515 + 3 + ccol + jt * 128 + 128],
                           id16[:], inc=(sP if kx == 7 else None))
                    wt(VEC, sP)
                    xtv = xtmc.rearrange("p (k d) -> p k d", d=128)
                    op(VEC, "tensor_copy", xtv[:, jt * 8:(jt + 1) * 8, :],
                       psXb[:].rearrange("p (k d) -> p k d", d=128), inc=sV)
                    wt(TEN, sV)
                for jt in range(2):
                    op(TEN, "transpose", psXb[:, 0:128],
                       xpre[:, 8 * 515 + 3 + ccol + jt * 128:
                            8 * 515 + 3 + ccol + jt * 128 + 128],
                       id16[:], inc=sP)
                    wt(VEC, sP)
                    op(VEC, "tensor_copy", btm[:, jt * 128:(jt + 1) * 128],
                       psXb[:, 0:128], inc=sV)
                    wt(TEN, sV)
                wt(VEC, sP)
                for jt in range(2):
                    cidx = ch * 2 + jt
                    xts = xtmc[:, jt * 1024:(jt + 1) * 1024].rearrange(
                        "p (h d) -> p h d", d=64)
                    xws = xwv[:, jt * 1024:(jt + 1) * 1024].rearrange(
                        "p (h d) -> p h d", d=64)
                    dds = ddtm[:, cidx * 64 + g * 16: cidx * 64 + g * 16 + 16]
                    op(VEC, "tensor_tensor", xws, xts,
                       dds.unsqueeze(2).broadcast_to([128, 16, 64]), ALU.mult,
                       inc=sV)

                y_guard = sV.n
                acs_guard_a = [sA.n, sA.n]
                acs_guard_v = [sV.n, sV.n]
                gt_guard = [sV.n, sV.n]
                st_guard = sV.n
                sdt_guard = sV.n
                lts_guard = sV.n
                csd_guard = sP.n
                rhsb_guard = sP.n
                first_update = True
                for pp in range(8):
                    h0 = 2 * pp
                    hhc = g * 16 + h0
                    pac = psAC[pp % 2]
                    wt(VEC, sP, rhsb_guard)
                    for jt in range(2):
                        cidx = ch * 2 + jt
                        ad2 = adttm[:, cidx * 64 + hhc: cidx * 64 + hhc + 2]
                        ltr = ltrow0 if jt == 0 else ltrow1
                        op(VEC, "tensor_tensor",
                           rhsb[:, jt * 512:(jt + 1) * 512].rearrange(
                               "p (s i) -> p s i", i=256),
                           ad2.unsqueeze(2).broadcast_to([128, 2, 256]),
                           ltr[:].unsqueeze(1).broadcast_to([128, 2, 256]),
                           ALU.mult, inc=sV)
                    rhsb_v = sV.n
                    wt(TEN, sV, rhsb_v)
                    wt(TEN, sA, acs_guard_a[pp % 2])
                    wt(TEN, sV, acs_guard_v[pp % 2])
                    op(TEN, "matmul", pac[:], ones128[:], rhsb[:, 0:512],
                       start=True, stop=False)
                    op(TEN, "matmul", pac[:], ones128[:], rhsb[:, 512:1024],
                       start=False, stop=True, inc=sP)
                    acsb_v = sP.n
                    rhsb_guard = sP.n
                    for s in range(2):
                        h = h0 + s
                        hh = g * 16 + h
                        scol = slice(s * 256, s * 256 + 256)
                        wt(ACT, sP, acsb_v)
                        wt(ACT, sV, sdt_guard)
                        op(ACT, "activation", sdtv, pac[:, scol], AF.Exp, inc=sA)
                        wt(VEC, sA)
                        wt(VEC, sP, csd_guard)
                        op(VEC, "tensor_tensor", csdv, xpre[:, ccsl], sdtv,
                           ALU.mult, inc=sV)
                        sdt_guard = sV.n
                        for jt in range(2):
                            cidx = ch * 2 + jt
                            op(VEC, "scalar_tensor_tensor",
                               ttsf[:, jt * 256:(jt + 1) * 256], pac[:, scol],
                               nacstm[:, cidx * 64 + hh: cidx * 64 + hh + 1],
                               maskbig[:, jt * 256:(jt + 1) * 256],
                               ALU.add, ALU.add, inc=sV)
                            wt(ACT, sV)
                            wt(ACT, sV, lts_guard)
                            op(ACT, "activation", lts[:, jt * 256:(jt + 1) * 256],
                               ttsf[:, jt * 256:(jt + 1) * 256], AF.Exp, inc=sA)
                        if s == 1:
                            acs_guard_a[pp % 2] = sA.n
                            acs_guard_v[pp % 2] = sV.n
                        pgt = psG[h % 2]
                        wt(TEN, sV, gt_guard[h % 2])
                        for jt in range(2):
                            bfm = xpre[:, 8 * 515 + 3 + ccol + jt * 128:
                                       8 * 515 + 3 + ccol + jt * 128 + 128]
                            op(TEN, "matmul", pgt[:, jt * 256:(jt + 1) * 256],
                               bfm, xpre[:, ccsl],
                               start=True, stop=True,
                               inc=(sP if jt == 1 else None))
                        gt_v = sP.n
                        wt(VEC, sP, gt_v)
                        wt(VEC, sA)
                        for jt in range(2):
                            cidx = ch * 2 + jt
                            op(VEC, "scalar_tensor_tensor",
                               glt[:, jt * 256:(jt + 1) * 256],
                               pgt[:, jt * 256:(jt + 1) * 256],
                               dttm[:, cidx * 64 + hh: cidx * 64 + hh + 1],
                               lts[:, jt * 256:(jt + 1) * 256],
                               ALU.mult, ALU.mult, inc=sV)
                        gt_guard[h % 2] = sV.n
                        glt_v = sV.n
                        lts_guard = sV.n
                        if debug and st == 0 and g == 0 and ch == 0 and h == 15:
                            wt(ACT, sV, glt_v)
                            dma(ACT, d_lt, lts, sS)
                            dma(ACT, d_gl, glt, sS)
                            dma(ACT, d_cs[:, 0:256], csdv, sS)
                            dma(ACT, d_cs[:, 256:512], sdtv, sS)
                            for _e in ENGS:
                                wt(_e, sS, sS.n)
                        ypsl = psY[s * 64:s * 64 + 64,
                                   (pp % 2) * 256:(pp % 2) * 256 + 256]
                        wt(TEN, sV, y_guard)
                        wt(TEN, sV, glt_v)
                        wt(TEN, sA, stab_v)
                        op(TEN, "matmul", ypsl,
                           xtmc[:, h * 64:h * 64 + 64], glt[:, 0:256],
                           start=True, stop=False)
                        op(TEN, "matmul", ypsl,
                           xtmc[:, 1024 + h * 64:1024 + h * 64 + 64],
                           glt[:, 256:512], start=False, stop=False)
                        op(TEN, "matmul", ypsl, stab[:, h * 64:h * 64 + 64],
                           csdv, start=False, stop=True, inc=sP)
                        y_v = sP.n
                        csd_guard = sP.n
                        wt(TEN, sV, st_guard)
                        stsl = psXf[:, (h % 4) * 64:(h % 4) * 64 + 64]
                        op(TEN, "matmul", stsl, btm[:, 0:128],
                           xwv[:, h * 64:h * 64 + 64], start=True, stop=False)
                        op(TEN, "matmul", stsl, btm[:, 128:256],
                           xwv[:, 1024 + h * 64:1024 + h * 64 + 64],
                           start=False, stop=True, inc=sP)
                        st_v = sP.n
                        wt(VEC, sP, st_v)
                        if first_update:
                            wt(VEC, sA, stab_v)
                            first_update = False
                        op(VEC, "scalar_tensor_tensor",
                           statef[:, hh * 64:hh * 64 + 64],
                           statef[:, hh * 64:hh * 64 + 64],
                           eatb[:, ch * 64 + hh: ch * 64 + hh + 1], stsl,
                           ALU.mult, ALU.add, inc=sV)
                        st_guard = sV.n
                        wt(VEC, sP, y_v)
                        kx = h // 2
                        xsl = xpre[s * 64:s * 64 + 64,
                                   kx * 515 + 3 + ccol: kx * 515 + 3 + ccol + 256]
                        dsl = dexp2[s * 64:s * 64 + 64, g * 8 + pp:g * 8 + pp + 1]
                        tmpb = sqt[s * 64:s * 64 + 64, :]
                        op(VEC, "scalar_tensor_tensor", tmpb, xsl, dsl, ypsl,
                           ALU.mult, ALU.add)
                        op(VEC, "tensor_tensor",
                           ygb[s * 64:s * 64 + 64, kx * 256:kx * 256 + 256],
                           tmpb,
                           sgb[s * 64:s * 64 + 64,
                               kx * 512 + ccol: kx * 512 + ccol + 256],
                           ALU.mult, inc=sV)
                        y_guard = sV.n
                ygb_v = sV.n
                if debug and st == 0 and g == 0 and ch == 0:
                    wt(ACT, sV, ygb_v)
                    dma(ACT, d_yg, ygb, sS)
                    for _e in ENGS:
                        wt(_e, sS, sS.n)
                sq_guard = sP.n
                wt(TEN, sV, st_guard)
                for ot in range(8):
                    wt(ACT, sV, ygb_v)
                    wt(ACT, sP, sq_guard)
                    op(ACT, "activation", sqt[:], ygb[:, ot * 256:(ot + 1) * 256],
                       AF.Square, inc=sA)
                    wt(TEN, sA)
                    op(TEN, "matmul", psXf[0:1, 256:512], onescol16[:], sqt[:],
                       start=(ot == 0), stop=(ot == 7), inc=sP)
                    sq_guard = sP.n
                var_v = sP.n
                wt(ACT, sP, var_v)
                wt(ACT, sP, rstd_guard[0])
                op(ACT, "activation", rstdr[0:1, 0:256], psXf[0:1, 256:512],
                   AF.Ln, bias=fconst[0:1, 0:1], scale=fconst[0:1, 2:3])
                op(ACT, "activation", rstdr[0:1, 256:512], rstdr[0:1, 0:256],
                   AF.Exp, scale=fconst[0:1, 3:4], inc=sA)
                wt(TEN, sA)
                op(TEN, "matmul", psXf[:, 256:512], onesf[:],
                   rstdr[0:1, 256:512], start=True, stop=True, inc=sP)
                rstdb_v = sP.n
                rstd_guard[0] = sP.n
                if debug and st == 0 and g == 0 and ch == 0:
                    wt(ACT, sP, rstdb_v)
                    op(ACT, "copy", convacc[:, 0:256], psXf[:, 256:512], inc=sA)
                    dma(ACT, d_nm[:, 0:256], convacc[:, 0:256], sS)
                    dma(ACT, d_nm[0:1, 256:768], rstdr[:], sS)
                    for _e in ENGS:
                        wt(_e, sS, sS.n)
                wt(VEC, sP, rstdb_v)
                if st > 0 and g == 0 and ch == 0:
                    wt(VEC, sP, v_yf_consumed)
                for ot in range(8):
                    kcy = g * 8 + ot
                    op(VEC, "scalar_tensor_tensor",
                       yf[:, kcy * 512 + ccol: kcy * 512 + ccol + 256],
                       ygb[:, ot * 256:(ot + 1) * 256],
                       nwc[:, kcy:kcy + 1], psXf[:, 256:512],
                       ALU.mult, ALU.mult, inc=sV)
        scan_done_v = sV.n
        scan_done_a = sA.n
        if debug and st == 0:
            wt(ACT, sV, scan_done_v)
            dma(ACT, d_yf, yf[:], sS)
            dma(ACT, d_st, statef[:], sS)
            for _e in ENGS:
                wt(_e, sS, sS.n)

        for ob in range(16):
            b = ob % 2
            wt(SYNC, sP, wbig_guard[b])
            if ob < 2:
                wt(SYNC, sV, scan_done_v)
                wt(SYNC, sA, scan_done_a)
            dma(SYNC, wbig[b],
                wopT[:, ob * 256:(ob + 1) * 256].rearrange(
                    "(kc p) o -> p kc o", p=128), sL)
            wv = sL.n
            for tt in range(4):
                idx = ob * 4 + tt
                ps = psA if idx % 2 == 0 else psB
                wt(TEN, sL, wv)
                wt(TEN, sA, evac_done[idx % 2])
                for kc in range(32):
                    op(TEN, "matmul", ps[:, 0:256],
                       yf[:, kc * 512 + tt * 128: kc * 512 + tt * 128 + 128],
                       wbig[b][:, kc * 256:(kc + 1) * 256],
                       start=(kc == 0), stop=(kc == 31),
                       inc=(sP if kc == 31 else None))
                mm_v = sP.n
                slot = idx % 4
                wt(ACT, sP, mm_v)
                wt(ACT, sS, sstage_guard[slot])
                if ob == 0 and tt == 0:
                    wt(ACT, sV, scan_done_v)
                op(ACT, "copy", sstage[:, slot * 256:(slot + 1) * 256],
                   ps[:, 0:256], inc=sA)
                evac_done[idx % 2] = sA.n
                half = ob // 8
                r = st * 1024 + half * 512 + tt * 128
                c = (ob % 8) * 256
                wt(SYNC, sA)
                dma(SYNC, s_part[r:r + 128, c:c + 256],
                    sstage[:, slot * 256:(slot + 1) * 256], sS)
                sstage_guard[slot] = sS.n
            wbig_guard[b] = sP.n
        op_stores = sS.n
        v_yf_consumed = sP.n

        if debug and st == 0:
            wt(GPS, sS, op_stores)
            dma(GPS, d_sp, s_part[0:1024, :], sS)
        wt(GPS, sS, op_stores)
        op(GPS, "collective_compute", "ReduceScatter", ALU.add,
           replica_groups=[[0, 1], [2, 3], [4, 5], [6, 7]],
           ins=[s_part[st * 1024:(st + 1) * 1024, :]],
           outs=[s_sum[st * 512:(st + 1) * 512, :]], inc=sCC)
        cc_v = sCC.n
        if debug and st == 0:
            wt(GPS, sCC, cc_v)
            dma(GPS, d_ss, s_sum[0:512, :], sS)

        for ob in range(8):
            b = ob % 2
            wt(SYNC, sP, wbig_guard[b])
            dma(SYNC, wbig[b],
                wfT[:, ob * 256:(ob + 1) * 256].rearrange(
                    "(kc p) o -> p kc o", p=128), sL)
            wv = sL.n
            for tt in range(4):
                idx = ob * 4 + tt
                ps = psA if idx % 2 == 0 else psB
                wt(TEN, sL, wv)
                wt(TEN, sA, evac_done[idx % 2])
                for kc in range(32):
                    op(TEN, "matmul", ps[:, 0:256],
                       xn[:, kc * 512 + tt * 128: kc * 512 + tt * 128 + 128],
                       wbig[b][:, kc * 256:(kc + 1) * 256],
                       start=(kc == 0), stop=False)
                op(TEN, "matmul", ps[:, 0:256], onesrow16[:],
                   fb16[0:1, ob * 256:(ob + 1) * 256], start=False, stop=True,
                   inc=sP)
                mm_v = sP.n
                slot = idx % 4
                wt(SYNC, sCC, cc_v)
                wt(SYNC, sV, gslot_guard[slot])
                r = st * 512 + tt * 128
                dma(SYNC, ssumb[:, slot * 256:(slot + 1) * 256],
                    s_sum[r:r + 128, ob * 256:(ob + 1) * 256], sL)
                dma(SYNC, xtm[:, slot * 256:(slot + 1) * 256],
                    xres[st * 512 + tt * 128: st * 512 + tt * 128 + 128,
                         ob * 256:(ob + 1) * 256], sL)
                ldv = sL.n
                wt(ACT, sP, mm_v)
                wt(ACT, sV, gslot_guard[slot])
                op(ACT, "activation", gbuf[:, slot * 256:(slot + 1) * 256],
                   ps[:, 0:256], AF.Silu, inc=sA)
                evac_done[idx % 2] = sA.n
                wt(VEC, sA)
                wt(VEC, sL, ldv)
                wt(VEC, sS, sstage_guard[slot])
                op(VEC, "tensor_tensor", sstage[:, slot * 256:(slot + 1) * 256],
                   gbuf[:, slot * 256:(slot + 1) * 256],
                   ssumb[:, slot * 256:(slot + 1) * 256], ALU.mult)
                wt(VEC, sS, obuf_guard[slot])
                op(VEC, "tensor_tensor", obuf[:, slot * 256:(slot + 1) * 256],
                   xtm[:, slot * 256:(slot + 1) * 256],
                   sstage[:, slot * 256:(slot + 1) * 256], ALU.add, inc=sV)
                gslot_guard[slot] = sV.n
                wt(ACT, sV)
                dma(ACT, outd[st * 512 + tt * 128: st * 512 + tt * 128 + 128,
                              ob * 256:(ob + 1) * 256],
                    obuf[:, slot * 256:(slot + 1) * 256], sS)
                obuf_guard[slot] = sS.n
            wbig_guard[b] = sP.n
        v_xn_consumed = sP.n
        v_scanreg_pe = sP.n
        v_sg_free = sV.n
        v_obuf_free = sS.n
        xtm_guard_v = sV.n

    wt(ACT, sS, sS.n)
    wt(GPS, sS, sS.n)
    wt(GPS, sCC, sCC.n)

    with nc.Block() as block:
        @block.sync
        def _(e):
            for fn in P[SYNC]:
                fn(e)

        @block.tensor
        def _(e):
            for fn in P[TEN]:
                fn(e)

        @block.vector
        def _(e):
            for fn in P[VEC]:
                fn(e)

        @block.scalar
        def _(e):
            for fn in P[ACT]:
                fn(e)

        @block.gpsimd
        def _(e):
            for fn in P[GPS]:
                fn(e)

    st_ctx.close()
    return nc


_NC = None


def _consts():
    c = {}
    j = np.arange(128)[:, None]
    i = np.arange(256)[None, :]
    c["ltrow0_i"] = (j <= i).astype(ml_dtypes.bfloat16)
    c["ltrow1_i"] = ((j + 128) <= i).astype(ml_dtypes.bfloat16)
    ii = np.arange(128)[None, :]
    c["tri16_i"] = (j <= ii).astype(ml_dtypes.bfloat16)
    c["ones128_i"] = np.ones((128, 128), dtype=ml_dtypes.bfloat16)
    mb = np.zeros((128, 512), dtype=np.float32)
    mb[:, 0:256] = np.where(j <= i, 0.0, -BIG)
    mb[:, 256:512] = np.where((j + 128) <= i, 0.0, -BIG)
    c["maskbig_i"] = mb
    c["id16_i"] = np.eye(128, dtype=ml_dtypes.bfloat16)
    c["idf_i"] = np.eye(128, dtype=np.float32)
    c["onesf_i"] = np.ones((1, 128), dtype=np.float32)
    c["onesrow16_i"] = np.ones((1, 128), dtype=ml_dtypes.bfloat16)
    c["onescol16_i"] = np.ones((128, 1), dtype=ml_dtypes.bfloat16)
    ud = np.zeros((2, 128), dtype=np.float32)
    ud[0, 0:64] = 1.0
    ud[1, 64:128] = 1.0
    c["updown_i"] = ud
    fc = np.zeros((128, 4), dtype=np.float32)
    fc[:, 0] = EPS
    fc[:, 1] = 1.0
    fc[:, 2] = 1.0 / 1024.0
    fc[:, 3] = -0.5
    c["fconst_i"] = fc
    return c


def _shard(inputs, c):
    b, H = c // 2, c % 2
    d = {}
    hsb = inputs["L0_attn_hidden_states"][b]
    d["hs"] = np.ascontiguousarray(hsb)
    d["xres"] = np.ascontiguousarray(hsb[:, 2048 * H:2048 * H + 2048])

    W = inputs["L0_attn_in_proj_w"]
    rows = []
    for g in range(4 * H, 4 * H + 4):
        rows.append(np.arange(1024 * g, 1024 * (g + 1)))
        rows.append(DIN + np.arange(1024 * g, 1024 * (g + 1)))
        rows.append(2 * DIN + np.arange(128 * g, 128 * (g + 1)))
        rows.append(2 * DIN + GTOT * NSD + np.arange(128 * g, 128 * (g + 1)))
    rows.append(2 * DIN + 2 * GTOT * NSD + np.arange(64 * H, 64 * H + 64))
    rows = np.concatenate(rows)
    wipc = W[rows]
    d["wip"] = np.ascontiguousarray(
        np.pad(wipc, ((0, WIP_R - wipc.shape[0]), (0, 0))))

    cw = inputs["L0_attn_conv_w"][:, 0, :]
    cb = inputs["L0_attn_conv_b"]
    crows = []
    for g in range(4 * H, 4 * H + 4):
        crows.append(np.arange(1024 * g, 1024 * (g + 1)))
        crows.append(DIN + np.arange(128 * g, 128 * (g + 1)))
        crows.append(DIN + GTOT * NSD + np.arange(128 * g, 128 * (g + 1)))
    crows = np.concatenate(crows)
    d["convw_i"] = np.ascontiguousarray(
        cw[crows].reshape(40, 128, 4).transpose(1, 0, 2).reshape(128, 160)
        .astype(np.float32))
    d["convb_i"] = np.ascontiguousarray(
        cb[crows].reshape(40, 128).T.astype(np.float32))

    d["dtb_i"] = np.ascontiguousarray(
        inputs["L0_attn_dt_bias"][64 * H:64 * H + 64].reshape(64, 1)
        .astype(np.float32))
    d["alog_i"] = np.ascontiguousarray(
        inputs["L0_attn_A_log"][64 * H:64 * H + 64].reshape(64, 1)
        .astype(np.float32))
    dv = inputs["L0_attn_D"][64 * H:64 * H + 64]
    d["dpair_i"] = np.ascontiguousarray(
        np.stack([dv[0::2], dv[1::2]]).astype(np.float32))
    nw = inputs["L0_attn_norm_w"][4096 * H:4096 * H + 4096]
    d["nw_i"] = np.ascontiguousarray(nw.reshape(32, 128).T.astype(np.float32))

    WO = inputs["L0_attn_out_proj_w"]
    cols = np.concatenate([np.arange(1024 * g, 1024 * (g + 1))
                           for g in range(4 * H, 4 * H + 4)])
    d["wop"] = np.ascontiguousarray(WO[:, cols])

    d["wq"] = np.ascontiguousarray(
        inputs["L0_ffn_qweight"][2048 * H:2048 * H + 2048])
    d["fbias"] = np.ascontiguousarray(
        inputs["L0_ffn_bias"][2048 * H:2048 * H + 2048].reshape(1, 2048)
        .astype(np.float32))
    d["scale_i"] = np.asarray(inputs["L0_ffn_scale"], dtype=np.float32).reshape(1, 1)
    d["zp_i"] = np.asarray(inputs["L0_ffn_zero_point"], dtype=np.float32).reshape(1, 1)
    return d


def kernel(**inputs):
    global _NC
    if _NC is None:
        _NC = build_nc()
    consts = _consts()
    in_maps = []
    for c in range(8):
        d = _shard(inputs, c)
        d.update(consts)
        in_maps.append(d)
    res = run_bass_kernel_spmd(_NC, in_maps, core_ids=list(range(8)))
    out = np.empty((4, L, HID), dtype=np.float32)
    for c in range(8):
        b, H = c // 2, c % 2
        out[b, :, 2048 * H:2048 * H + 2048] = res.results[c]["out"]
    return out


# version 03 (final)
# speedup vs baseline: 1.0028x; 1.0028x over previous
import numpy as np
import ml_dtypes
from contextlib import ExitStack

import concourse.bass as bass
import concourse.mybir as mybir
from concourse.bass_utils import run_bass_kernel_spmd

F32 = mybir.dt.float32
BF16 = mybir.dt.bfloat16
I32 = mybir.dt.int32
AF = mybir.ActivationFunctionType
ALU = mybir.AluOpType

L = 4096
HID = 4096
TST = 512
NST = L // TST
CS = 256
NG = 4
HPG = 16
NHC = NG * HPG
DIN = 8192
GTOT = 8
NSD = 128
WIP_R = 9344
GROW = 2304
EPS = 1e-5
BIG = 1.0e9

SYNC, TEN, VEC, ACT, GPS = "sync", "tensor", "vector", "scalar", "gpsimd"
ENGS = [SYNC, TEN, VEC, ACT, GPS]


class Sem:
    def __init__(self, h):
        self.h = h
        self.n = 0


def build_nc(debug=False):
    nc = bass.Bass("TRN2", target_bir_lowering=False)

    def ein(name, shape, dt):
        return nc.dram_tensor(name, shape, dt, kind="ExternalInput").ap()

    hs = ein("hs", [L, HID], F32)
    xres = ein("xres", [L, 2048], F32)
    wip = ein("wip", [WIP_R, HID], F32)
    wop = ein("wop", [HID, 4096], F32)
    wq = ein("wq", [2048, HID], I32)
    fbias = ein("fbias", [1, 2048], F32)
    scale_i = ein("scale_i", [1, 1], F32)
    zp_i = ein("zp_i", [1, 1], F32)
    convw_i = ein("convw_i", [128, 160], F32)
    convb_i = ein("convb_i", [128, 40], F32)
    nw_i = ein("nw_i", [128, 32], F32)
    dtb_i = ein("dtb_i", [64, 1], F32)
    alog_i = ein("alog_i", [64, 1], F32)
    dpair_i = ein("dpair_i", [2, 32], F32)
    ltrow0_i = ein("ltrow0_i", [128, 256], BF16)
    ltrow1_i = ein("ltrow1_i", [128, 256], BF16)
    tri16_i = ein("tri16_i", [128, 128], BF16)
    ones128_i = ein("ones128_i", [128, 128], BF16)
    maskbig_i = ein("maskbig_i", [128, 512], F32)
    id16_i = ein("id16_i", [128, 128], BF16)
    idf_i = ein("idf_i", [128, 128], F32)
    onesf_i = ein("onesf_i", [1, 128], F32)
    onesrow16_i = ein("onesrow16_i", [1, 128], BF16)
    onescol16_i = ein("onescol16_i", [128, 1], BF16)
    updown_i = ein("updown_i", [2, 128], F32)
    fconst_i = ein("fconst_i", [128, 4], F32)

    outd = nc.dram_tensor("out", [L, 2048], F32, kind="ExternalOutput").ap()
    if debug:
        d_xn = nc.dram_tensor("d_xn", [128, 16384], BF16, kind="ExternalOutput").ap()
        d_dt = nc.dram_tensor("d_dt", [64, 1024], F32, kind="ExternalOutput").ap()
        d_pc = nc.dram_tensor("d_pc", [128, 1024], F32, kind="ExternalOutput").ap()
        d_xp = nc.dram_tensor("d_xp", [128, 5150], BF16, kind="ExternalOutput").ap()
        d_sg = nc.dram_tensor("d_sg", [128, 4096], BF16, kind="ExternalOutput").ap()
        d_yf = nc.dram_tensor("d_yf", [128, 16384], BF16, kind="ExternalOutput").ap()
        d_sp = nc.dram_tensor("d_sp", [1024, 2048], BF16, kind="ExternalOutput").ap()
        d_ss = nc.dram_tensor("d_ss", [512, 2048], BF16, kind="ExternalOutput").ap()
        d_st = nc.dram_tensor("d_st", [128, 4096], F32, kind="ExternalOutput").ap()
        d_yg = nc.dram_tensor("d_yg", [128, 2048], BF16, kind="ExternalOutput").ap()
        d_nm = nc.dram_tensor("d_nm", [128, 1024], F32, kind="ExternalOutput").ap()
        d_wf = nc.dram_tensor("d_wf", [256, 2048], BF16, kind="ExternalOutput").ap()
        d_wq = nc.dram_tensor("d_wq", [2048, 128], BF16, kind="ExternalOutput").ap()
        d_sq = nc.dram_tensor("d_sq", [128, 4096], BF16, kind="ExternalOutput").ap()
        d_cb = nc.dram_tensor("d_cb", [128, 1536], F32, kind="ExternalOutput").ap()
        d_wfull = nc.dram_tensor("d_wfull", [4096, 2048], BF16, kind="ExternalOutput").ap()
        d_wfull2 = nc.dram_tensor("d_wfull2", [2048, 2048], BF16, kind="ExternalOutput").ap()
        d_lt = nc.dram_tensor("d_lt", [128, 512], BF16, kind="ExternalOutput").ap()
        d_gl = nc.dram_tensor("d_gl", [128, 512], BF16, kind="ExternalOutput").ap()
        d_cs = nc.dram_tensor("d_cs", [128, 512], BF16, kind="ExternalOutput").ap()

    wip_bf = nc.dram_tensor("wip_bf", [WIP_R, HID], BF16).ap()
    wipT = nc.dram_tensor("wipT", [HID, WIP_R], BF16).ap()
    wop_bf = nc.dram_tensor("wop_bf", [HID, 4096], BF16).ap()
    wopT = nc.dram_tensor("wopT", [HID, 4096], BF16).ap()
    wq_bf = nc.dram_tensor("wq_bf", [2048, HID], BF16).ap()
    wfT = nc.dram_tensor("wfT", [HID, 2048], BF16).ap()
    fb_bf = nc.dram_tensor("fb_bf", [1, 2048], BF16).ap()
    s_part = nc.dram_tensor("s_part", [NST * 1024, 2048], BF16).ap()
    s_sum = nc.dram_tensor("s_sum", [NST * 512, 2048], BF16).ap()

    st_ctx = ExitStack()

    def sbuf(name, shape, dt):
        return st_ctx.enter_context(nc.sbuf_tensor(name, shape, dt))

    xn = sbuf("xn", [128, 32 * 512], BF16)
    yf = sbuf("yf", [128, 32 * 512], BF16)
    xtm = sbuf("xtm", [128, 4096], F32)
    xntm = sbuf("xntm", [128, 4096], BF16)
    scanreg = sbuf("scanreg", [128, 16384], BF16)
    wsec = sbuf("wsec", [128, 8192], BF16)
    xpre = sbuf("xpre", [128, 10 * 515], BF16)
    tails = sbuf("tails", [128, 120], BF16)
    sgb = sbuf("sgb", [128, 8 * 512], BF16)
    wdt = sbuf("wdt", [128, 32 * 64], BF16)
    dtsp = sbuf("dtsp", [64, 512], F32)
    tmp1 = sbuf("tmp1", [64, 512], F32)
    tmp2 = sbuf("tmp2", [64, 512], F32)
    adt = sbuf("adt", [64, 512], F32)
    adttm = sbuf("adttm", [128, 256], BF16)
    dttm = sbuf("dttm", [128, 256], F32)
    acstm = sbuf("acstm", [128, 256], F32)
    nacstm = sbuf("nacstm", [128, 256], F32)
    ddtm = sbuf("ddtm", [128, 256], BF16)
    ddt1 = sbuf("ddt1", [128, 64], F32)
    ddt2 = sbuf("ddt2", [128, 64], F32)
    eatb = sbuf("eatb", [128, 128], F32)
    rstdr = sbuf("rstdr", [1, 512], F32)
    statef = sbuf("statef", [128, NHC * 64], F32)
    ltrow0 = sbuf("ltrow0", [128, 256], BF16)
    ltrow1 = sbuf("ltrow1", [128, 256], BF16)
    tri16 = sbuf("tri16", [128, 128], BF16)
    ones128 = sbuf("ones128", [128, 128], BF16)
    maskbig = sbuf("maskbig", [128, 512], F32)
    id16 = sbuf("id16", [128, 128], BF16)
    idf = sbuf("idf", [128, 128], F32)
    onesf = sbuf("onesf", [1, 128], F32)
    onesrow16 = sbuf("onesrow16", [1, 128], BF16)
    onescol16 = sbuf("onescol16", [128, 1], BF16)
    updown = sbuf("updown", [2, 128], F32)
    dpair = sbuf("dpair", [2, 32], F32)
    nwc = sbuf("nwc", [128, 32], F32)
    cwc = sbuf("cwc", [128, 160], F32)
    cbc = sbuf("cbc", [128, 40], F32)
    dtbc = sbuf("dtbc", [64, 1], F32)
    alogc = sbuf("alogc", [64, 1], F32)
    expalog = sbuf("expalog", [64, 1], F32)
    dexp2 = sbuf("dexp2", [128, 32], F32)
    scl = sbuf("scl", [1, 1], F32)
    zpv = sbuf("zpv", [1, 1], F32)
    scaleb = sbuf("scaleb", [128, 1], F32)
    zpb = sbuf("zpb", [128, 1], F32)
    biascol = sbuf("biascol", [128, 1], F32)
    fb16 = sbuf("fb16", [1, 2048], BF16)
    stats = sbuf("stats", [128, 48], F32)
    mv = sbuf("mv", [128, 2], F32)
    rstdc = sbuf("rstdc", [128, 1], F32)
    nbias = sbuf("nbias", [128, 1], F32)
    lntmp = sbuf("lntmp", [128, 1], F32)
    fconst = sbuf("fconst", [128, 4], F32)

    xtmc = scanreg[:, 0:2048]
    xwv = scanreg[:, 2048:4096]
    ygb = scanreg[:, 4096:6144]
    glt = scanreg[:, 6144:6656]
    lts = scanreg[:, 6656:7168]
    ttsf = scanreg[:, 7168:9216].bitcast(F32)
    csdv = scanreg[:, 9216:9472]
    sdtv = scanreg[:, 9472:9728]
    stab = scanreg[:, 9728:10752]
    btm = scanreg[:, 10752:11008]
    sqt = scanreg[:, 11008:11264]
    convacc = scanreg[:, 11264:13312].bitcast(F32)
    rhsb = scanreg[:, 13312:14336]
    glt_s = [glt, scanreg[:, 14336:14848]]
    lts_s = [lts, scanreg[:, 14848:15360]]
    csd_s = [csdv, scanreg[:, 15360:15616]]
    sdt_s = [sdtv, scanreg[:, 15616:15872]]
    wbig = [scanreg[:, 0:8192], scanreg[:, 8192:16384]]
    sstage = sgb[:, 0:1024]
    gbuf = sgb[:, 1024:2048]
    ssumb = sgb[:, 2048:3072]
    obuf = xpre[:, 0:2048].bitcast(F32)

    psA = st_ctx.enter_context(nc.psum_tensor("psA", [128, 512], F32))
    psB = st_ctx.enter_context(nc.psum_tensor("psB", [128, 512], F32))
    psAC = [st_ctx.enter_context(nc.psum_tensor("psAC0", [128, 512], F32)),
            st_ctx.enter_context(nc.psum_tensor("psAC1", [128, 512], F32))]
    psG = [st_ctx.enter_context(nc.psum_tensor("psG0", [128, 512], F32)),
           st_ctx.enter_context(nc.psum_tensor("psG1", [128, 512], F32))]
    psY = st_ctx.enter_context(nc.psum_tensor("psY", [128, 512], F32))
    psX = st_ctx.enter_context(nc.psum_tensor("psX", [128, 512], F32))
    psXb = psX[:].bitcast(BF16)

    sems = [Sem(st_ctx.enter_context(nc.semaphore(s)))
            for s in ["sL", "sS", "sC", "sCC", "sP", "sV", "sA"]]
    sL, sS, sC, sCC, sP, sV, sA = sems

    P = {e: [] for e in ENGS}

    def op(eng, meth, *a, inc=None, incn=1, **kw):
        if inc is not None:
            inc.n += incn
            h, n = inc.h, incn
        else:
            h, n = None, 0

        def fn(e, meth=meth, a=a, kw=kw, h=h, n=n):
            ins = getattr(e, meth)(*a, **kw)
            if h is not None:
                ins.then_inc(h, n)
        P[eng].append(fn)
        return inc.n if inc else None

    def wt(eng, sem, val=None):
        v = sem.n if val is None else val

        def fn(e, h=sem.h, v=v):
            e.wait_ge(h, v)
        P[eng].append(fn)

    def dma(eng, out, in_, sem, **kw):
        return op(eng, "dma_start", inc=sem, incn=16, out=out, in_=in_, **kw)

    for dst, src in [
        (ltrow0[:], ltrow0_i), (ltrow1[:], ltrow1_i), (tri16[:], tri16_i),
        (ones128[:], ones128_i), (maskbig[:], maskbig_i), (id16[:], id16_i),
        (idf[:], idf_i), (onesf[:], onesf_i), (onesrow16[:], onesrow16_i),
        (onescol16[:], onescol16_i), (updown[:], updown_i), (dpair[:], dpair_i),
        (nwc[:], nw_i), (cwc[:], convw_i), (cbc[:], convb_i),
        (dtbc[:], dtb_i), (alogc[:], alog_i), (scl[:], scale_i), (zpv[:], zp_i),
        (fconst[:], fconst_i),
    ]:
        dma(SYNC, dst, src, sL)
    consts_loaded = sL.n

    op(VEC, "memset", statef[:], 0.0)
    op(VEC, "memset", tails[:], 0.0, inc=sV)
    memset_v = sV.n

    wt(ACT, sL, consts_loaded)
    op(ACT, "activation", expalog[:], alogc[:], AF.Exp, inc=sA)
    wt(TEN, sL, consts_loaded)
    psXf = psX[:]
    op(TEN, "matmul", psXf[:, 0:1], onesf[:], scl[:], start=True, stop=True)
    op(TEN, "matmul", psXf[:, 1:2], onesf[:], zpv[:], start=True, stop=True)
    op(TEN, "matmul", psXf[:, 2:34], updown[:], dpair[:], start=True, stop=True,
       inc=sP)
    wt(ACT, sP)
    op(ACT, "copy", scaleb[:], psXf[:, 0:1])
    op(ACT, "copy", zpb[:], psXf[:, 1:2])
    op(ACT, "copy", dexp2[:], psXf[:, 2:34], inc=sA)
    psx_setup_done = sA.n
    wt(VEC, sA, psx_setup_done)
    op(VEC, "tensor_scalar", biascol[:], scaleb[:], zpb[:], -1.0,
       ALU.mult, ALU.mult, inc=sV)
    biascol_v = sV.n

    for i in range(8):
        dma(GPS, wip_bf[i * 1168:(i + 1) * 1168, :], wip[i * 1168:(i + 1) * 1168, :], sC)
    cast_wip = sC.n
    for i in range(4):
        dma(GPS, wop_bf[i * 1024:(i + 1) * 1024, :], wop[i * 1024:(i + 1) * 1024, :], sC)
    cast_wop = sC.n
    for i in range(2):
        dma(GPS, wq_bf[i * 1024:(i + 1) * 1024, :], wq[i * 1024:(i + 1) * 1024, :], sC)
    cast_wq = sC.n
    dma(GPS, fb_bf, fbias, sC)
    cast_fb = sC.n

    wt(SYNC, sC, cast_fb)
    dma(SYNC, fb16[:], fb_bf, sL)

    def prepass_transpose(wsrc, wdst, nob, cast_v, dequant=False):
        cbuf = yf[:, 0:nob * 128]
        rbuf = xn[:, 0:nob * 128]
        rbuf2 = xn[:, 8192:8192 + nob * 128] if dequant else None
        wt(SYNC, sC, cast_v)
        for kc in range(32):
            wt(SYNC, sP)
            dma(SYNC, cbuf,
                wsrc[:, kc * 128:(kc + 1) * 128].rearrange(
                    "(ob p) k -> p ob k", p=128), sL)
            ldv = sL.n
            wt(TEN, sL, ldv)
            nbat = (nob + 7) // 8
            for bq in range(nbat):
                o0 = bq * 8
                o1 = min(o0 + 8, nob)
                wt(TEN, sV)
                for ob in range(o0, o1):
                    op(TEN, "transpose", psXb[:, (ob - o0) * 128:(ob - o0 + 1) * 128],
                       cbuf[:, ob * 128:(ob + 1) * 128], id16[:],
                       inc=(sP if ob == o1 - 1 else None))
                wt(VEC, sP)
                if bq == 0:
                    wt(VEC, sS)
                op(VEC, "tensor_copy", rbuf[:, o0 * 128:o1 * 128],
                   psXb[:, 0:(o1 - o0) * 128], inc=sV)
            if dequant:
                wt(ACT, sV)
                op(ACT, "activation", rbuf2, rbuf, AF.Identity,
                   bias=biascol[:], scale=scaleb[:], inc=sA)
                wt(SYNC, sA)
                dma(SYNC, wdst[kc * 128:(kc + 1) * 128, :], rbuf2, sS)
            else:
                wt(ACT, sV)
                dma(ACT, wdst[kc * 128:(kc + 1) * 128, :], rbuf, sS)

    wt(ACT, sV, biascol_v)
    prepass_transpose(wip_bf, wipT, 73, cast_wip)
    prepass_transpose(wop_bf, wopT, 32, cast_wop)
    prepass_transpose(wq_bf, wfT, 16, cast_wq, dequant=True)
    prepass_stores = sS.n
    if debug:
        wt(GPS, sS, prepass_stores)
        dma(GPS, d_wf[0:128, :], wfT[0:128, :], sS)
        dma(GPS, d_wf[128:256, :], wfT[1024:1152, :], sS)
        dma(GPS, d_wq, wq_bf[:, 0:128], sS)
        dma(GPS, d_wfull2, wq_bf[:, 2048:4096], sS)
        dma(GPS, d_wfull, wfT, sS)

    for e in ENGS:
        wt(e, sS, prepass_stores)
        wt(e, sV, memset_v)

    v_yf_consumed = 0
    v_xn_consumed = 0
    v_sg_free = 0
    v_obuf_free = 0
    v_scanreg_pe = 0
    xtm_guard_a = 0
    xtm_guard_v = 0
    wsec_guard = [0, 0]
    wbig_guard = [0, 0]
    sstage_guard = [0, 0, 0, 0]
    gslot_guard = [0, 0, 0, 0]
    obuf_guard = [0, 0, 0, 0]
    evac_done = [0, 0]
    dtmm_prev = [0]
    rstd_guard = [0]

    for st in range(NST):
        r0 = st * TST
        for tt in range(4):
            rows = slice(r0 + tt * 128, r0 + tt * 128 + 128)
            wt(SYNC, sA, xtm_guard_a)
            wt(SYNC, sV, xtm_guard_v)
            dma(SYNC, xtm[:, 0:2048], hs[rows, 0:2048], sL)
            dma(SYNC, xtm[:, 2048:4096], hs[rows, 2048:4096], sL)
            xldv = sL.n
            wt(VEC, sL, xldv)
            wt(VEC, sA)
            for i in range(8):
                op(VEC, "bn_stats", stats[:, 6 * i:6 * i + 6],
                   xtm[:, 512 * i:512 * i + 512])
            op(VEC, "bn_aggr", mv[:], stats[:], inc=sV)
            xtm_guard_v = sV.n
            wt(ACT, sV)
            op(ACT, "activation", lntmp[:], mv[:, 1:2], AF.Ln, bias=fconst[:, 0:1])
            op(ACT, "activation", rstdc[:], lntmp[:], AF.Exp, scale=fconst[:, 3:4], inc=sA)
            wt(VEC, sA)
            op(VEC, "tensor_scalar", nbias[:], mv[:, 0:1], rstdc[:], -1.0,
               ALU.mult, ALU.mult, inc=sV)
            wt(ACT, sV)
            wt(ACT, sP)
            op(ACT, "activation", xntm[:, 0:2048], xtm[:, 0:2048], AF.Identity,
               bias=nbias[:], scale=rstdc[:])
            op(ACT, "activation", xntm[:, 2048:4096], xtm[:, 2048:4096], AF.Identity,
               bias=nbias[:], scale=rstdc[:], inc=sA)
            xtm_guard_a = sA.n
            xn_ready_a = sA.n
            for bq in range(4):
                wt(TEN, sA, xn_ready_a)
                wt(TEN, sV)
                for k in range(8):
                    kc = bq * 8 + k
                    op(TEN, "transpose", psXb[:, k * 128:(k + 1) * 128],
                       xntm[:, kc * 128:(kc + 1) * 128], id16[:],
                       inc=(sP if k == 7 else None))
                if st > 0 and tt == 0 and bq == 0:
                    wt(VEC, sP, v_xn_consumed)
                wt(VEC, sP)
                xnv = xn[:].rearrange("p (kc t) -> p kc t", t=512)
                op(VEC, "tensor_copy",
                   xnv[:, bq * 8:(bq + 1) * 8, tt * 128:(tt + 1) * 128],
                   psXb[:].rearrange("p (k t) -> p k t", t=128), inc=sV)
        xn_done_v = sV.n
        if debug and st == 0:
            wt(ACT, sV, xn_done_v)
            dma(ACT, d_xn, xn[:], sS)
            for _e in ENGS:
                wt(_e, sS, sS.n)

        wt(SYNC, sP, dtmm_prev[0])
        dma(SYNC, wdt[:], wipT[:, 9216:9280].rearrange("(kc p) o -> p kc o", p=128), sL)
        wdt_v = sL.n
        wt(TEN, sL, wdt_v)
        wt(TEN, sV, xn_done_v)
        wt(TEN, sA, evac_done[0])
        for kc in range(32):
            op(TEN, "matmul", psA[0:64, :], wdt[:, kc * 64:(kc + 1) * 64],
               xn[:, kc * 512:(kc + 1) * 512], start=(kc == 0), stop=(kc == 31),
               inc=(sP if kc == 31 else None))
        dtmm_prev[0] = sP.n
        wt(ACT, sP)
        op(ACT, "activation", tmp1[:], psA[0:64, :], AF.Relu, bias=dtbc[:])
        op(ACT, "activation", tmp2[:], psA[0:64, :], AF.Abs, bias=dtbc[:], inc=sA)
        evac_done[0] = sA.n
        op(ACT, "activation", dtsp[:], tmp2[:], AF.Exp, scale=-1.0)
        op(ACT, "activation", tmp2[:], dtsp[:], AF.Ln, bias=fconst[0:64, 1:2], inc=sA)
        wt(VEC, sA)
        op(VEC, "tensor_tensor", dtsp[:], tmp1[:], tmp2[:], ALU.add)
        op(VEC, "tensor_scalar", adt[:], dtsp[:], expalog[:], -1.0,
           ALU.mult, ALU.mult, inc=sV)

        for ch in range(2):
            c2 = ch * 2
            wt(TEN, sV)
            for jt in range(2):
                cidx = c2 + jt
                csl = slice(ch * 256 + jt * 128, ch * 256 + jt * 128 + 128)
                op(TEN, "transpose", psXf[:, 0:64], dtsp[0:64, csl],
                   idf[0:64, 0:64])
                op(TEN, "transpose", psXf[:, 64:128], adt[0:64, csl],
                   idf[0:64, 0:64], inc=sP)
                wt(ACT, sP)
                op(ACT, "copy", dttm[:, cidx * 64:(cidx + 1) * 64], psXf[:, 0:64])
                op(ACT, "copy", adttm[:, cidx * 64:(cidx + 1) * 64],
                   psXf[:, 64:128], inc=sA)
                wt(TEN, sA)
            op(TEN, "matmul", psXf[:, 0:64], tri16[:],
               adttm[:, c2 * 64:(c2 + 1) * 64], start=True, stop=True)
            op(TEN, "matmul", psXf[:, 64:128], ones128[:],
               adttm[:, c2 * 64:(c2 + 1) * 64], start=True, stop=False)
            op(TEN, "matmul", psXf[:, 64:128], tri16[:],
               adttm[:, (c2 + 1) * 64:(c2 + 2) * 64], start=False, stop=True)
            op(TEN, "matmul", psXf[:, 128:192], ones128[:],
               adttm[:, c2 * 64:(c2 + 1) * 64], start=True, stop=False)
            op(TEN, "matmul", psXf[:, 128:192], ones128[:],
               adttm[:, (c2 + 1) * 64:(c2 + 2) * 64], start=False, stop=True,
               inc=sP)
            wt(ACT, sP)
            op(ACT, "copy", acstm[:, c2 * 64:(c2 + 2) * 64], psXf[:, 0:128])
            op(ACT, "activation", eatb[:, ch * 64:(ch + 1) * 64],
               psXf[:, 128:192], AF.Exp, inc=sA)
            wt(VEC, sA)
            op(VEC, "tensor_scalar", nacstm[:, c2 * 64:(c2 + 2) * 64],
               acstm[:, c2 * 64:(c2 + 2) * 64], -1.0, None, ALU.mult)
            for jt in range(2):
                cidx = c2 + jt
                op(VEC, "tensor_tensor", ddt1[:], psXf[:, 128:192],
                   acstm[:, cidx * 64:(cidx + 1) * 64], ALU.subtract, inc=sV)
                wt(ACT, sV)
                op(ACT, "activation", ddt2[:], ddt1[:], AF.Exp, inc=sA)
                wt(VEC, sA)
                op(VEC, "tensor_tensor", ddtm[:, cidx * 64:(cidx + 1) * 64],
                   ddt2[:], dttm[:, cidx * 64:(cidx + 1) * 64], ALU.mult, inc=sV)
            wt(TEN, sV)
        if debug and st == 0:
            wt(ACT, sV)
            dma(ACT, d_dt[:, 0:512], dtsp[:], sS)
            dma(ACT, d_dt[:, 512:1024], adt[:], sS)
            dma(ACT, d_pc[:, 0:256], acstm[:], sS)
            dma(ACT, d_pc[:, 256:384], eatb[:], sS)
            dma(ACT, d_pc[:, 384:640], dttm[:], sS)
            dma(ACT, d_pc[:, 640:768], adttm[:].bitcast(F32), sS)
            dma(ACT, d_pc[:, 768:896], ddtm[:].bitcast(F32), sS)
            for _e in ENGS:
                wt(_e, sS, sS.n)

        for g in range(NG):
            gcol0 = g * GROW
            if st > 0 and g == 0:
                wt(ACT, sP, v_scanreg_pe)
                wt(VEC, sP, v_scanreg_pe)
            for ot in range(18):
                b = ot % 2
                wt(SYNC, sP, wsec_guard[b])
                c0 = gcol0 + ot * 128
                dma(SYNC, wsec[:, b * 4096:(b + 1) * 4096],
                    wipT[:, c0:c0 + 128].rearrange("(kc p) o -> p kc o", p=128), sL)
                wv = sL.n
                ps = psA if b == 0 else psB
                wt(TEN, sL, wv)
                wt(TEN, sA, evac_done[b])
                for kc in range(32):
                    op(TEN, "matmul", ps[:],
                       wsec[:, b * 4096 + kc * 128: b * 4096 + (kc + 1) * 128],
                       xn[:, kc * 512:(kc + 1) * 512],
                       start=(kc == 0), stop=(kc == 31),
                       inc=(sP if kc == 31 else None))
                wsec_guard[b] = sP.n
                wt(ACT, sP)
                if ot == 0:
                    wt(ACT, sV)
                if ot < 8:
                    if st > 0 and g == 0 and ot == 0:
                        wt(ACT, sV, v_sg_free)
                    op(ACT, "activation", sgb[:, ot * 512:(ot + 1) * 512], ps[:],
                       AF.Silu, inc=sA)
                else:
                    cot = ot - 8
                    if st > 0 and g == 0 and ot == 8:
                        wt(ACT, sS, v_obuf_free)
                    op(ACT, "copy", xpre[:, cot * 515 + 3:cot * 515 + 515], ps[:],
                       inc=sA)
                evac_done[b] = sA.n
            wt(VEC, sA)
            for cot in range(10):
                blk = cot * 515
                ci = g * 10 + cot
                wt(VEC, sA)
                op(VEC, "tensor_copy", xpre[:, blk:blk + 3],
                   tails[:, ci * 3:ci * 3 + 3])
                op(VEC, "tensor_scalar", convacc[:, 0:512],
                   xpre[:, blk + 3:blk + 515],
                   cwc[:, ci * 4 + 3:ci * 4 + 4], cbc[:, ci:ci + 1],
                   ALU.mult, ALU.add)
                op(VEC, "scalar_tensor_tensor", convacc[:, 512:1024],
                   xpre[:, blk + 2:blk + 514], cwc[:, ci * 4 + 2:ci * 4 + 3],
                   convacc[:, 0:512], ALU.mult, ALU.add)
                op(VEC, "scalar_tensor_tensor", convacc[:, 0:512],
                   xpre[:, blk + 1:blk + 513], cwc[:, ci * 4 + 1:ci * 4 + 2],
                   convacc[:, 512:1024], ALU.mult, ALU.add)
                op(VEC, "scalar_tensor_tensor", convacc[:, 512:1024],
                   xpre[:, blk + 0:blk + 512], cwc[:, ci * 4 + 0:ci * 4 + 1],
                   convacc[:, 0:512], ALU.mult, ALU.add)
                op(VEC, "tensor_copy", tails[:, ci * 3:ci * 3 + 3],
                   xpre[:, blk + 512:blk + 515], inc=sV)
                wt(ACT, sV)
                op(ACT, "activation", xpre[:, blk + 3:blk + 515],
                   convacc[:, 512:1024], AF.Silu, inc=sA)
            conv_done = sA.n
            if debug and st == 0 and g == 0:
                dma(ACT, d_xp, xpre[:], sS)
                dma(ACT, d_sg, sgb[:], sS)
                for _e in ENGS:
                    wt(_e, sS, sS.n)

            for ch in range(2):
                ccol = ch * 256
                ccsl = slice(9 * 515 + 3 + ccol, 9 * 515 + 3 + ccol + 256)
                wt(ACT, sV)
                wt(ACT, sP)
                op(ACT, "copy", stab, statef[:, g * 1024:(g + 1) * 1024], inc=sA)
                stab_v = sA.n
                wt(TEN, sA, conv_done)
                wt(TEN, sV)
                for jt in range(2):
                    for kx in range(8):
                        op(TEN, "transpose", psXb[:, kx * 128:(kx + 1) * 128],
                           xpre[:, kx * 515 + 3 + ccol + jt * 128:
                                kx * 515 + 3 + ccol + jt * 128 + 128],
                           id16[:], inc=(sP if kx == 7 else None))
                    wt(VEC, sP)
                    xtv = xtmc.rearrange("p (k d) -> p k d", d=128)
                    op(VEC, "tensor_copy", xtv[:, jt * 8:(jt + 1) * 8, :],
                       psXb[:].rearrange("p (k d) -> p k d", d=128), inc=sV)
                    wt(TEN, sV)
                for jt in range(2):
                    op(TEN, "transpose", psXb[:, 0:128],
                       xpre[:, 8 * 515 + 3 + ccol + jt * 128:
                            8 * 515 + 3 + ccol + jt * 128 + 128],
                       id16[:], inc=sP)
                    wt(VEC, sP)
                    op(VEC, "tensor_copy", btm[:, jt * 128:(jt + 1) * 128],
                       psXb[:, 0:128], inc=sV)
                    wt(TEN, sV)
                wt(VEC, sP)
                for jt in range(2):
                    cidx = ch * 2 + jt
                    xts = xtmc[:, jt * 1024:(jt + 1) * 1024].rearrange(
                        "p (h d) -> p h d", d=64)
                    xws = xwv[:, jt * 1024:(jt + 1) * 1024].rearrange(
                        "p (h d) -> p h d", d=64)
                    dds = ddtm[:, cidx * 64 + g * 16: cidx * 64 + g * 16 + 16]
                    op(VEC, "tensor_tensor", xws, xts,
                       dds.unsqueeze(2).broadcast_to([128, 16, 64]), ALU.mult,
                       inc=sV)

                y_guard = sV.n
                acs_guard_a = [sA.n, sA.n]
                acs_guard_v = [sV.n, sV.n]
                gt_guard = [sV.n, sV.n]
                st_guard = sV.n
                sdt_guard = [sV.n, sV.n]
                lts_guard = [sV.n, sV.n]
                tts_guard = [sA.n, sA.n]
                ycons_guard = [sP.n, sP.n]
                rhsb_guard = sP.n
                first_update = True
                for pp in range(8):
                    h0 = 2 * pp
                    hhc = g * 16 + h0
                    pac = psAC[pp % 2]
                    wt(VEC, sP, rhsb_guard)
                    for jt in range(2):
                        cidx = ch * 2 + jt
                        ad2 = adttm[:, cidx * 64 + hhc: cidx * 64 + hhc + 2]
                        ltr = ltrow0 if jt == 0 else ltrow1
                        op(VEC, "tensor_tensor",
                           rhsb[:, jt * 512:(jt + 1) * 512].rearrange(
                               "p (s i) -> p s i", i=256),
                           ad2.unsqueeze(2).broadcast_to([128, 2, 256]),
                           ltr[:].unsqueeze(1).broadcast_to([128, 2, 256]),
                           ALU.mult, inc=sV)
                    rhsb_v = sV.n
                    wt(TEN, sV, rhsb_v)
                    wt(TEN, sA, acs_guard_a[pp % 2])
                    wt(TEN, sV, acs_guard_v[pp % 2])
                    op(TEN, "matmul", pac[:], ones128[:], rhsb[:, 0:512],
                       start=True, stop=False)
                    op(TEN, "matmul", pac[:], ones128[:], rhsb[:, 512:1024],
                       start=False, stop=True, inc=sP)
                    acsb_v = sP.n
                    rhsb_guard = sP.n
                    for s in range(2):
                        h = h0 + s
                        hh = g * 16 + h
                        hp = h % 2
                        scol = slice(s * 256, s * 256 + 256)
                        wt(ACT, sP, acsb_v)
                        wt(ACT, sV, sdt_guard[hp])
                        op(ACT, "activation", sdt_s[hp], pac[:, scol], AF.Exp,
                           inc=sA)
                        wt(VEC, sA)
                        wt(VEC, sP, ycons_guard[hp])
                        op(VEC, "tensor_tensor", csd_s[hp], xpre[:, ccsl],
                           sdt_s[hp], ALU.mult, inc=sV)
                        sdt_guard[hp] = sV.n
                        wt(VEC, sA, tts_guard[hp])
                        for jt in range(2):
                            cidx = ch * 2 + jt
                            op(VEC, "scalar_tensor_tensor",
                               ttsf[:, hp * 512 + jt * 256: hp * 512 + (jt + 1) * 256],
                               pac[:, scol],
                               nacstm[:, cidx * 64 + hh: cidx * 64 + hh + 1],
                               maskbig[:, jt * 256:(jt + 1) * 256],
                               ALU.add, ALU.add, inc=sV)
                        wt(ACT, sV)
                        wt(ACT, sV, lts_guard[hp])
                        op(ACT, "activation", lts_s[hp],
                           ttsf[:, hp * 512: hp * 512 + 512], AF.Exp, inc=sA)
                        tts_guard[hp] = sA.n
                        if s == 1:
                            acs_guard_a[pp % 2] = sA.n
                            acs_guard_v[pp % 2] = sV.n
                        pgt = psG[h % 2]
                        wt(TEN, sV, gt_guard[h % 2])
                        for jt in range(2):
                            bfm = xpre[:, 8 * 515 + 3 + ccol + jt * 128:
                                       8 * 515 + 3 + ccol + jt * 128 + 128]
                            op(TEN, "matmul", pgt[:, jt * 256:(jt + 1) * 256],
                               bfm, xpre[:, ccsl],
                               start=True, stop=True,
                               inc=(sP if jt == 1 else None))
                        gt_v = sP.n
                        wt(VEC, sP, gt_v)
                        wt(VEC, sA)
                        for jt in range(2):
                            cidx = ch * 2 + jt
                            op(VEC, "scalar_tensor_tensor",
                               glt_s[hp][:, jt * 256:(jt + 1) * 256],
                               pgt[:, jt * 256:(jt + 1) * 256],
                               dttm[:, cidx * 64 + hh: cidx * 64 + hh + 1],
                               lts_s[hp][:, jt * 256:(jt + 1) * 256],
                               ALU.mult, ALU.mult, inc=sV)
                        gt_guard[h % 2] = sV.n
                        glt_v = sV.n
                        lts_guard[hp] = sV.n
                        ypsl = psY[s * 64:s * 64 + 64,
                                   (pp % 2) * 256:(pp % 2) * 256 + 256]
                        wt(TEN, sV, y_guard)
                        wt(TEN, sV, glt_v)
                        wt(TEN, sA, stab_v)
                        op(TEN, "matmul", ypsl,
                           xtmc[:, h * 64:h * 64 + 64], glt_s[hp][:, 0:256],
                           start=True, stop=False)
                        op(TEN, "matmul", ypsl,
                           xtmc[:, 1024 + h * 64:1024 + h * 64 + 64],
                           glt_s[hp][:, 256:512], start=False, stop=False)
                        op(TEN, "matmul", ypsl, stab[:, h * 64:h * 64 + 64],
                           csd_s[hp], start=False, stop=True, inc=sP)
                        y_v = sP.n
                        ycons_guard[hp] = sP.n
                        wt(TEN, sV, st_guard)
                        stsl = psXf[:, (h % 4) * 64:(h % 4) * 64 + 64]
                        op(TEN, "matmul", stsl, btm[:, 0:128],
                           xwv[:, h * 64:h * 64 + 64], start=True, stop=False)
                        op(TEN, "matmul", stsl, btm[:, 128:256],
                           xwv[:, 1024 + h * 64:1024 + h * 64 + 64],
                           start=False, stop=True, inc=sP)
                        st_v = sP.n
                        wt(VEC, sP, st_v)
                        if first_update:
                            wt(VEC, sA, stab_v)
                            first_update = False
                        op(VEC, "scalar_tensor_tensor",
                           statef[:, hh * 64:hh * 64 + 64],
                           statef[:, hh * 64:hh * 64 + 64],
                           eatb[:, ch * 64 + hh: ch * 64 + hh + 1], stsl,
                           ALU.mult, ALU.add, inc=sV)
                        st_guard = sV.n
                        wt(VEC, sP, y_v)
                        kx = h // 2
                        xsl = xpre[s * 64:s * 64 + 64,
                                   kx * 515 + 3 + ccol: kx * 515 + 3 + ccol + 256]
                        dsl = dexp2[s * 64:s * 64 + 64, g * 8 + pp:g * 8 + pp + 1]
                        tmpb = sqt[s * 64:s * 64 + 64, :]
                        op(VEC, "scalar_tensor_tensor", tmpb, xsl, dsl, ypsl,
                           ALU.mult, ALU.add)
                        op(VEC, "tensor_tensor",
                           ygb[s * 64:s * 64 + 64, kx * 256:kx * 256 + 256],
                           tmpb,
                           sgb[s * 64:s * 64 + 64,
                               kx * 512 + ccol: kx * 512 + ccol + 256],
                           ALU.mult, inc=sV)
                        y_guard = sV.n
                ygb_v = sV.n
                if debug and st == 0 and g == 0 and ch == 0:
                    wt(ACT, sV, ygb_v)
                    dma(ACT, d_yg, ygb, sS)
                    for _e in ENGS:
                        wt(_e, sS, sS.n)
                sq_guard = sP.n
                wt(TEN, sV, st_guard)
                for ot in range(8):
                    wt(ACT, sV, ygb_v)
                    wt(ACT, sP, sq_guard)
                    op(ACT, "activation", sqt[:], ygb[:, ot * 256:(ot + 1) * 256],
                       AF.Square, inc=sA)
                    wt(TEN, sA)
                    op(TEN, "matmul", psXf[0:1, 256:512], onescol16[:], sqt[:],
                       start=(ot == 0), stop=(ot == 7), inc=sP)
                    sq_guard = sP.n
                var_v = sP.n
                wt(ACT, sP, var_v)
                wt(ACT, sP, rstd_guard[0])
                op(ACT, "activation", rstdr[0:1, 0:256], psXf[0:1, 256:512],
                   AF.Ln, bias=fconst[0:1, 0:1], scale=fconst[0:1, 2:3])
                op(ACT, "activation", rstdr[0:1, 256:512], rstdr[0:1, 0:256],
                   AF.Exp, scale=fconst[0:1, 3:4], inc=sA)
                wt(TEN, sA)
                op(TEN, "matmul", psXf[:, 256:512], onesf[:],
                   rstdr[0:1, 256:512], start=True, stop=True, inc=sP)
                rstdb_v = sP.n
                rstd_guard[0] = sP.n
                if debug and st == 0 and g == 0 and ch == 0:
                    wt(ACT, sP, rstdb_v)
                    op(ACT, "copy", convacc[:, 0:256], psXf[:, 256:512], inc=sA)
                    dma(ACT, d_nm[:, 0:256], convacc[:, 0:256], sS)
                    dma(ACT, d_nm[0:1, 256:768], rstdr[:], sS)
                    for _e in ENGS:
                        wt(_e, sS, sS.n)
                wt(VEC, sP, rstdb_v)
                if st > 0 and g == 0 and ch == 0:
                    wt(VEC, sP, v_yf_consumed)
                for ot in range(8):
                    kcy = g * 8 + ot
                    op(VEC, "scalar_tensor_tensor",
                       yf[:, kcy * 512 + ccol: kcy * 512 + ccol + 256],
                       ygb[:, ot * 256:(ot + 1) * 256],
                       nwc[:, kcy:kcy + 1], psXf[:, 256:512],
                       ALU.mult, ALU.mult, inc=sV)
        scan_done_v = sV.n
        scan_done_a = sA.n
        if debug and st == 0:
            wt(ACT, sV, scan_done_v)
            dma(ACT, d_yf, yf[:], sS)
            dma(ACT, d_st, statef[:], sS)
            for _e in ENGS:
                wt(_e, sS, sS.n)

        for ob in range(16):
            b = ob % 2
            wt(SYNC, sP, wbig_guard[b])
            if ob < 2:
                wt(SYNC, sV, scan_done_v)
                wt(SYNC, sA, scan_done_a)
            dma(SYNC, wbig[b],
                wopT[:, ob * 256:(ob + 1) * 256].rearrange(
                    "(kc p) o -> p kc o", p=128), sL)
            wv = sL.n
            for tt in range(4):
                idx = ob * 4 + tt
                ps = psA if idx % 2 == 0 else psB
                wt(TEN, sL, wv)
                wt(TEN, sA, evac_done[idx % 2])
                for kc in range(32):
                    op(TEN, "matmul", ps[:, 0:256],
                       yf[:, kc * 512 + tt * 128: kc * 512 + tt * 128 + 128],
                       wbig[b][:, kc * 256:(kc + 1) * 256],
                       start=(kc == 0), stop=(kc == 31),
                       inc=(sP if kc == 31 else None))
                mm_v = sP.n
                slot = idx % 4
                wt(ACT, sP, mm_v)
                wt(ACT, sS, sstage_guard[slot])
                if ob == 0 and tt == 0:
                    wt(ACT, sV, scan_done_v)
                op(ACT, "copy", sstage[:, slot * 256:(slot + 1) * 256],
                   ps[:, 0:256], inc=sA)
                evac_done[idx % 2] = sA.n
                half = ob // 8
                r = st * 1024 + half * 512 + tt * 128
                c = (ob % 8) * 256
                wt(SYNC, sA)
                dma(SYNC, s_part[r:r + 128, c:c + 256],
                    sstage[:, slot * 256:(slot + 1) * 256], sS)
                sstage_guard[slot] = sS.n
            wbig_guard[b] = sP.n
        op_stores = sS.n
        v_yf_consumed = sP.n

        if debug and st == 0:
            wt(GPS, sS, op_stores)
            dma(GPS, d_sp, s_part[0:1024, :], sS)
        wt(GPS, sS, op_stores)
        op(GPS, "collective_compute", "ReduceScatter", ALU.add,
           replica_groups=[[0, 1], [2, 3], [4, 5], [6, 7]],
           ins=[s_part[st * 1024:(st + 1) * 1024, :]],
           outs=[s_sum[st * 512:(st + 1) * 512, :]], inc=sCC)
        cc_v = sCC.n
        if debug and st == 0:
            wt(GPS, sCC, cc_v)
            dma(GPS, d_ss, s_sum[0:512, :], sS)

        for ob in range(8):
            b = ob % 2
            wt(SYNC, sP, wbig_guard[b])
            dma(SYNC, wbig[b],
                wfT[:, ob * 256:(ob + 1) * 256].rearrange(
                    "(kc p) o -> p kc o", p=128), sL)
            wv = sL.n
            for tt in range(4):
                idx = ob * 4 + tt
                ps = psA if idx % 2 == 0 else psB
                wt(TEN, sL, wv)
                wt(TEN, sA, evac_done[idx % 2])
                for kc in range(32):
                    op(TEN, "matmul", ps[:, 0:256],
                       xn[:, kc * 512 + tt * 128: kc * 512 + tt * 128 + 128],
                       wbig[b][:, kc * 256:(kc + 1) * 256],
                       start=(kc == 0), stop=False)
                op(TEN, "matmul", ps[:, 0:256], onesrow16[:],
                   fb16[0:1, ob * 256:(ob + 1) * 256], start=False, stop=True,
                   inc=sP)
                mm_v = sP.n
                slot = idx % 4
                wt(SYNC, sCC, cc_v)
                wt(SYNC, sV, gslot_guard[slot])
                r = st * 512 + tt * 128
                dma(SYNC, ssumb[:, slot * 256:(slot + 1) * 256],
                    s_sum[r:r + 128, ob * 256:(ob + 1) * 256], sL)
                dma(SYNC, xtm[:, slot * 256:(slot + 1) * 256],
                    xres[st * 512 + tt * 128: st * 512 + tt * 128 + 128,
                         ob * 256:(ob + 1) * 256], sL)
                ldv = sL.n
                wt(ACT, sP, mm_v)
                wt(ACT, sV, gslot_guard[slot])
                op(ACT, "activation", gbuf[:, slot * 256:(slot + 1) * 256],
                   ps[:, 0:256], AF.Silu, inc=sA)
                evac_done[idx % 2] = sA.n
                wt(VEC, sA)
                wt(VEC, sL, ldv)
                wt(VEC, sS, sstage_guard[slot])
                op(VEC, "tensor_tensor", sstage[:, slot * 256:(slot + 1) * 256],
                   gbuf[:, slot * 256:(slot + 1) * 256],
                   ssumb[:, slot * 256:(slot + 1) * 256], ALU.mult)
                wt(VEC, sS, obuf_guard[slot])
                op(VEC, "tensor_tensor", obuf[:, slot * 256:(slot + 1) * 256],
                   xtm[:, slot * 256:(slot + 1) * 256],
                   sstage[:, slot * 256:(slot + 1) * 256], ALU.add, inc=sV)
                gslot_guard[slot] = sV.n
                wt(ACT, sV)
                dma(ACT, outd[st * 512 + tt * 128: st * 512 + tt * 128 + 128,
                              ob * 256:(ob + 1) * 256],
                    obuf[:, slot * 256:(slot + 1) * 256], sS)
                obuf_guard[slot] = sS.n
            wbig_guard[b] = sP.n
        v_xn_consumed = sP.n
        v_scanreg_pe = sP.n
        v_sg_free = sV.n
        v_obuf_free = sS.n
        xtm_guard_v = sV.n

    wt(ACT, sS, sS.n)
    wt(GPS, sS, sS.n)
    wt(GPS, sCC, sCC.n)

    with nc.Block() as block:
        @block.sync
        def _(e):
            for fn in P[SYNC]:
                fn(e)

        @block.tensor
        def _(e):
            for fn in P[TEN]:
                fn(e)

        @block.vector
        def _(e):
            for fn in P[VEC]:
                fn(e)

        @block.scalar
        def _(e):
            for fn in P[ACT]:
                fn(e)

        @block.gpsimd
        def _(e):
            for fn in P[GPS]:
                fn(e)

    st_ctx.close()
    return nc


_NC = None


def _consts():
    c = {}
    j = np.arange(128)[:, None]
    i = np.arange(256)[None, :]
    c["ltrow0_i"] = (j <= i).astype(ml_dtypes.bfloat16)
    c["ltrow1_i"] = ((j + 128) <= i).astype(ml_dtypes.bfloat16)
    ii = np.arange(128)[None, :]
    c["tri16_i"] = (j <= ii).astype(ml_dtypes.bfloat16)
    c["ones128_i"] = np.ones((128, 128), dtype=ml_dtypes.bfloat16)
    mb = np.zeros((128, 512), dtype=np.float32)
    mb[:, 0:256] = np.where(j <= i, 0.0, -BIG)
    mb[:, 256:512] = np.where((j + 128) <= i, 0.0, -BIG)
    c["maskbig_i"] = mb
    c["id16_i"] = np.eye(128, dtype=ml_dtypes.bfloat16)
    c["idf_i"] = np.eye(128, dtype=np.float32)
    c["onesf_i"] = np.ones((1, 128), dtype=np.float32)
    c["onesrow16_i"] = np.ones((1, 128), dtype=ml_dtypes.bfloat16)
    c["onescol16_i"] = np.ones((128, 1), dtype=ml_dtypes.bfloat16)
    ud = np.zeros((2, 128), dtype=np.float32)
    ud[0, 0:64] = 1.0
    ud[1, 64:128] = 1.0
    c["updown_i"] = ud
    fc = np.zeros((128, 4), dtype=np.float32)
    fc[:, 0] = EPS
    fc[:, 1] = 1.0
    fc[:, 2] = 1.0 / 1024.0
    fc[:, 3] = -0.5
    c["fconst_i"] = fc
    return c


def _shard(inputs, c):
    b, H = c // 2, c % 2
    d = {}
    hsb = inputs["L0_attn_hidden_states"][b]
    d["hs"] = np.ascontiguousarray(hsb)
    d["xres"] = np.ascontiguousarray(hsb[:, 2048 * H:2048 * H + 2048])

    W = inputs["L0_attn_in_proj_w"]
    rows = []
    for g in range(4 * H, 4 * H + 4):
        rows.append(np.arange(1024 * g, 1024 * (g + 1)))
        rows.append(DIN + np.arange(1024 * g, 1024 * (g + 1)))
        rows.append(2 * DIN + np.arange(128 * g, 128 * (g + 1)))
        rows.append(2 * DIN + GTOT * NSD + np.arange(128 * g, 128 * (g + 1)))
    rows.append(2 * DIN + 2 * GTOT * NSD + np.arange(64 * H, 64 * H + 64))
    rows = np.concatenate(rows)
    wipc = W[rows]
    d["wip"] = np.ascontiguousarray(
        np.pad(wipc, ((0, WIP_R - wipc.shape[0]), (0, 0))))

    cw = inputs["L0_attn_conv_w"][:, 0, :]
    cb = inputs["L0_attn_conv_b"]
    crows = []
    for g in range(4 * H, 4 * H + 4):
        crows.append(np.arange(1024 * g, 1024 * (g + 1)))
        crows.append(DIN + np.arange(128 * g, 128 * (g + 1)))
        crows.append(DIN + GTOT * NSD + np.arange(128 * g, 128 * (g + 1)))
    crows = np.concatenate(crows)
    d["convw_i"] = np.ascontiguousarray(
        cw[crows].reshape(40, 128, 4).transpose(1, 0, 2).reshape(128, 160)
        .astype(np.float32))
    d["convb_i"] = np.ascontiguousarray(
        cb[crows].reshape(40, 128).T.astype(np.float32))

    d["dtb_i"] = np.ascontiguousarray(
        inputs["L0_attn_dt_bias"][64 * H:64 * H + 64].reshape(64, 1)
        .astype(np.float32))
    d["alog_i"] = np.ascontiguousarray(
        inputs["L0_attn_A_log"][64 * H:64 * H + 64].reshape(64, 1)
        .astype(np.float32))
    dv = inputs["L0_attn_D"][64 * H:64 * H + 64]
    d["dpair_i"] = np.ascontiguousarray(
        np.stack([dv[0::2], dv[1::2]]).astype(np.float32))
    nw = inputs["L0_attn_norm_w"][4096 * H:4096 * H + 4096]
    d["nw_i"] = np.ascontiguousarray(nw.reshape(32, 128).T.astype(np.float32))

    WO = inputs["L0_attn_out_proj_w"]
    cols = np.concatenate([np.arange(1024 * g, 1024 * (g + 1))
                           for g in range(4 * H, 4 * H + 4)])
    d["wop"] = np.ascontiguousarray(WO[:, cols])

    d["wq"] = np.ascontiguousarray(
        inputs["L0_ffn_qweight"][2048 * H:2048 * H + 2048])
    d["fbias"] = np.ascontiguousarray(
        inputs["L0_ffn_bias"][2048 * H:2048 * H + 2048].reshape(1, 2048)
        .astype(np.float32))
    d["scale_i"] = np.asarray(inputs["L0_ffn_scale"], dtype=np.float32).reshape(1, 1)
    d["zp_i"] = np.asarray(inputs["L0_ffn_zero_point"], dtype=np.float32).reshape(1, 1)
    return d


def kernel(**inputs):
    global _NC
    if _NC is None:
        _NC = build_nc()
    consts = _consts()
    in_maps = []
    for c in range(8):
        d = _shard(inputs, c)
        d.update(consts)
        in_maps.append(d)
    res = run_bass_kernel_spmd(_NC, in_maps, core_ids=list(range(8)))
    out = np.empty((4, L, HID), dtype=np.float32)
    for c in range(8):
        b, H = c // 2, c % 2
        out[b, :, 2048 * H:2048 * H + 2048] = res.results[c]["out"]
    return out
